# version 22
# baseline (speedup 1.0000x reference)
import math
import os
import numpy as np
import ml_dtypes
import concourse.bass as bass
import concourse.mybir as mybir
from concourse.bass_utils import run_bass_kernel_spmd
from contextlib import ExitStack
from types import SimpleNamespace

F32 = mybir.dt.float32
BF16 = mybir.dt.bfloat16
ALU = mybir.AluOpType
AF = mybir.ActivationFunctionType
AXX = mybir.AxisListType.X

D = 1024
DC = 8
TPC = 2048
TH = 1024
TTW = 512
NTT = 2
NE = 8
SINT = 2048
SEQ = 4096
EPS = 1e-5
SLOT = 1024
INV_D = 1.0 / D


class Sem:
    def __init__(self, ctx, nc, name):
        self.h = ctx.enter_context(nc.semaphore(name))
        self.n = 0
        self.name = name
        self._w = {}

    def inc(self, inst, k=1):
        inst.then_inc(self.h, k)
        self.n += k
        return self.n

    def wait(self, eng, n=None):
        n = self.n if n is None else n
        assert n <= self.n, f"wait {self.name} {n} > {self.n}"
        if n <= 0:
            return
        k = id(eng)
        if self._w.get(k, -1) >= n:
            return
        eng.wait_ge(self.h, n)
        self._w[k] = n


class WRing:

    def __init__(self, S):
        self.S = S
        nc, ctx = S.nc, S.ctx
        self.nstage = 2
        self.nslot = 4
        self.stage = [ctx.enter_context(
            nc.sbuf_tensor(f"wstage{i}", [128, SLOT], F32))
            for i in range(self.nstage)]
        self.hi = [ctx.enter_context(
            nc.sbuf_tensor(f"whi{i}", [128, SLOT], BF16))
            for i in range(self.nslot)]
        self.lo = [ctx.enter_context(
            nc.sbuf_tensor(f"wlo{i}", [128, SLOT], BF16))
            for i in range(self.nslot)]
        self.fills = []
        self.nfill = 0
        self.nsplit = 0
        self.split_ve = {}
        self.use_te = {}
        self.init_ve = S.s_ve.n

    def queue(self, dram_ap):
        self.fills.append(dram_ap)
        return len(self.fills) - 1

    def _fill_ok(self):
        s = self.nfill
        if s >= len(self.fills):
            return False
        if s - self.nsplit >= self.nstage:
            return False
        if s - self.nstage >= 0 and s - self.nstage >= self.nsplit:
            return False
        return True

    def _emit_fill(self):
        s = self.nfill
        S = self.S
        eng = S.nc.sync
        if s - self.nstage >= 0:
            S.s_ve.wait(eng, self.split_ve[s - self.nstage])
        else:
            S.s_ve.wait(eng, self.init_ve)
        src = self.fills[s]
        dst = self.stage[s % 2][:].rearrange("p (a b) -> p a b",
                                             a=src.shape[1])
        S.s_wdma.inc(eng.dma_start(out=dst, in_=src), 16)
        self.nfill += 1

    def _emit_split(self):
        s = self.nsplit
        S = self.S
        while self.nfill <= s:
            assert self._fill_ok()
            self._emit_fill()
        v = S.nc.vector
        S.s_wdma.wait(v, 16 * (s + 1))
        if s - self.nslot >= 0:
            S.s_te.wait(v, self.use_te[s - self.nslot])
        st = self.stage[s % self.nstage][:]
        v.tensor_copy(self.hi[s % self.nslot][:], st)
        inst = v.tensor_tensor(self.lo[s % self.nslot][:], st,
                               self.hi[s % self.nslot][:], ALU.subtract)
        self.split_ve[s] = S.s_ve.inc(inst)
        self.nsplit += 1

    def _split_ok(self):
        s = self.nsplit
        if s >= len(self.fills):
            return False
        if s - self.nslot >= 0 and (s - self.nslot) not in self.use_te:
            return False
        if s >= self.nfill and not self._fill_ok():
            return False
        return True

    def prefetch(self, ahead=1):
        tgt = self.nsplit + ahead
        while self.nsplit < tgt and self._split_ok():
            self._emit_split()
        while self._fill_ok() and self.nfill <= self.nsplit + 2:
            self._emit_fill()

    def acquire(self, slots):
        hi = max(slots)
        assert hi - min(slots) <= 1 and hi < len(self.fills)
        while self.nsplit <= hi:
            self._emit_split()
        self.prefetch(2)
        while self._fill_ok() and self.nfill <= hi + 3:
            self._emit_fill()
        for s in slots:
            self.S.s_ve.wait(self.S.nc.tensor, self.split_ve[s])

    def view(self, s, a):
        h = self.hi[s % self.nslot][:].rearrange("p (a b) -> p a b", a=a)
        l = self.lo[s % self.nslot][:].rearrange("p (a b) -> p a b", a=a)
        return h, l

    def release(self, slots):
        for s in slots:
            self.use_te[s] = self.S.s_te.n
        self.prefetch(2)


class PsumRing:

    def __init__(self, S, names):
        self.S = S
        self.names = names
        self.i = 0

    def begin(self):
        nm = self.names[self.i % len(self.names)]
        self._nm = nm
        t = self.S.ptok.get(nm)
        if t is not None:
            t[0].wait(self.S.nc.tensor, t[1])
        return self.S.banks[nm]

    def done(self, sem, n):
        self.S.ptok[self._nm] = (sem, n)
        self.i += 1


def p_begin(S, name):
    t = S.ptok.get(name)
    if t is not None:
        t[0].wait(S.nc.tensor, t[1])
    return S.banks[name] if name in getattr(S, "banks", {}) else getattr(S, name)


def p_done(S, name, sem, n):
    S.ptok[name] = (sem, n)


def mm3_group(S, p_ap, chunks):
    t = S.nc.tensor
    seq = []
    for (lh, ll, rh, rl) in chunks:
        seq.append((lh, rh))
        if rl is not None:
            seq.append((lh, rl))
        if ll is not None:
            seq.append((ll, rh))
    n = len(seq)
    inst = None
    for i, (lhsT, rhs) in enumerate(seq):
        inst = t.matmul(p_ap, lhsT, rhs, start=(i == 0), stop=(i == n - 1))
    return inst


def emit_layernorm(S, th):
    nc = S.nc
    t, v, sc = nc.tensor, nc.vector, nc.scalar
    S.s_ve.wait(t, S.x_guard)
    S.s_ve.wait(sc, S.x_guard)
    se_stat = [0, 0]
    for tt in range(NTT):
        base = th * TH + tt * TTW
        xs = [S.X[:, dc, base:base + TTW] for dc in range(DC)]
        p_begin(S, "pS")
        for dc in range(DC):
            inst = t.matmul(S.pS[0:1, 0:TTW], S.ones_col_f[:, 0:1], xs[dc],
                            start=(dc == 0), stop=(dc == DC - 1))
        S.s_te.inc(inst)
        te_sq = {}
        for dc in range(DC):
            if dc >= 2:
                S.s_te.wait(sc, te_sq[dc - 2])
            elif S.sq_guard:
                S.s_te.wait(sc, S.sq_guard)
            inst = sc.activation(S.sq[dc % 2][:], xs[dc], AF.Square)
            se_sq = S.s_se.inc(inst)
            S.s_se.wait(t, se_sq)
            inst = t.matmul(S.pS[0:1, TTW:2 * TTW], S.ones_col_f[:, 0:1],
                            S.sq[dc % 2][:], start=(dc == 0),
                            stop=(dc == DC - 1))
            te_sq[dc] = S.s_te.inc(inst)
        S.sq_guard = te_sq[DC - 1]
        S.s_ve.wait(sc, S.lnstat_guard)
        S.s_te.wait(sc, te_sq[DC - 1])
        inst = sc.copy(S.stat[0:1, tt * 1024:tt * 1024 + 2 * TTW],
                       S.pS[0:1, 0:2 * TTW])
        se_stat[tt] = S.s_se.inc(inst)
        p_done(S, "pS", S.s_se, se_stat[tt])
    for tt in range(NTT):
        base = th * TH + tt * TTW
        ucol = tt * TTW
        xs = [S.X[:, dc, base:base + TTW] for dc in range(DC)]
        sm = S.sm[0:1, 0:TTW]
        sv = S.sv[0:1, 0:TTW]
        sw = S.sw[0:1, 0:TTW]
        si = S.si[0:1, 0:TTW]
        st = S.stat[0:1, tt * 1024:tt * 1024 + 2 * TTW]
        S.s_se.wait(v, se_stat[tt])
        v.tensor_scalar_mul(sm, st[0:1, 0:TTW], INV_D)
        v.tensor_scalar_mul(sv, st[0:1, TTW:2 * TTW], INV_D)
        v.drain()
        v.tensor_mul(sw, sm, sm)
        v.drain()
        v.tensor_sub(sv, sv, sw)
        v.drain()
        inst = v.tensor_scalar_add(sv, sv, EPS)
        ve_r = S.s_ve.inc(inst)
        S.lnstat_guard = ve_r
        S.s_ve.wait(sc, ve_r)
        if S.ln_bc_guard:
            S.s_te.wait(sc, S.ln_bc_guard)
        sc.activation(sw, sv, AF.Ln)
        inst = sc.activation(si, sw, AF.Exp, scale=-0.5)
        se_istd = S.s_se.inc(inst)
        S.s_se.wait(t, se_istd)
        p_begin(S, "pe")
        t.matmul(S.pE[:, 0:TTW], S.ones_row_f[0:1, :], sm,
                 start=True, stop=True)
        p_begin(S, "pf")
        inst = t.matmul(S.pF[:, 0:TTW], S.ones_row_f[0:1, :], si,
                        start=True, stop=True)
        te_bc = S.s_te.inc(inst)
        S.ln_bc_guard = te_bc
        S.s_te.wait(v, te_bc)
        for dc in range(DC):
            v.tensor_sub(S.tmpA[:], xs[dc], S.pE[:, 0:TTW])
            v.tensor_mul(S.tmpB[:], S.tmpA[:], S.pF[:, 0:TTW])
            v.tensor_copy(S.u_hi[:, dc, ucol:ucol + TTW], S.tmpB[:])
            inst = v.tensor_tensor(S.u_lo[:, dc, ucol:ucol + TTW], S.tmpB[:],
                                   S.u_hi[:, dc, ucol:ucol + TTW],
                                   ALU.subtract)
        ve_u = S.s_ve.inc(inst)
        p_done(S, "pe", S.s_ve, ve_u)
        p_done(S, "pf", S.s_ve, ve_u)
        S.u_ready[tt] = ve_u


def emit_gate(S, th):
    nc = S.nc
    t, v, sc = nc.tensor, nc.vector, nc.scalar
    for tt in range(NTT):
        ucol = tt * TTW
        p_begin(S, "pf")
        chunks = []
        for dc in range(DC):
            chunks.append((S.gw_hi[:, dc, :], S.gw_lo[:, dc, :],
                           S.u_hi[:, dc, ucol:ucol + TTW],
                           S.u_lo[:, dc, ucol:ucol + TTW]))
        S.s_ve.wait(t, S.u_ready[tt])
        inst = mm3_group(S, S.pF[0:8, 0:TTW], chunks)
        te_log = S.s_te.inc(inst)
        if S.gate_guard:
            S.s_te.wait(sc, S.gate_guard)
        S.s_te.wait(sc, te_log)
        inst = sc.activation(S.glog[0:8, 0:TTW], S.pF[0:8, 0:TTW], AF.Identity,
                             bias=S.gb[0:8, 0:1])
        se_log = S.s_se.inc(inst)
        S.s_se.wait(t, se_log)
        p_begin(S, "pe")
        for ts in range(4):
            inst = t.matmul(S.pE[:, ts * 8:(ts + 1) * 8],
                            S.glog[0:8, ts * 128:(ts + 1) * 128],
                            S.ident_f[0:8, 0:8], is_transpose=True,
                            start=(ts == 0), stop=(ts == 3))
        te_tr = S.s_te.inc(inst)
        p_done(S, "pf", S.s_se, se_log)
        S.s_ve.wait(sc, S.gate_guard_ve)
        S.s_te.wait(sc, te_tr)
        inst = sc.activation(S.ge[:, 0:32], S.pE[:, 0:32], AF.Exp)
        se_e = S.s_se.inc(inst)
        p_done(S, "pe", S.s_se, se_e)
        if S.gate_guard:
            S.s_te.wait(v, S.gate_guard)
        S.s_se.wait(v, se_e)
        geg = S.ge[:, 0:32].rearrange("p (g e) -> p g e", g=4)
        v.tensor_reduce(S.g_l[:, 0:4], geg, AXX, ALU.add)
        v.tensor_reduce(S.g_m1[:, 0:4], geg, AXX, ALU.max)
        v.drain()
        v.reciprocal(S.g_rl[:, 0:4], S.g_l[:, 0:4])
        for ts in range(4):
            e_ts = S.ge[:, ts * 8:(ts + 1) * 8]
            v.drain()
            v.tensor_scalar(S.g_eq[:, 0:8], e_ts, S.g_m1[:, ts:ts + 1], None,
                            ALU.is_equal)
            v.drain()
            v.scalar_tensor_tensor(S.g_wo[:, 0:8], S.g_eq[:, 0:8], -1e30,
                                   e_ts, ALU.mult, ALU.add)
            v.drain()
            v.tensor_reduce(S.g_m2[:, 0:1], S.g_wo[:, 0:8], AXX, ALU.max)
            v.drain()
            v.tensor_scalar(S.g_eq[:, 0:8], e_ts, S.g_m2[:, 0:1], None,
                            ALU.is_ge)
            v.drain()
            v.tensor_mul(S.g_gu[:, 0:8], e_ts, S.g_eq[:, 0:8])
            v.drain()
            inst = v.tensor_scalar_mul(S.gtm[:, ts * 8:(ts + 1) * 8],
                                       S.g_gu[:, 0:8], S.g_rl[:, ts:ts + 1])
        ve_g = S.s_ve.inc(inst)
        S.gate_guard_ve = ve_g
        S.s_ve.wait(t, ve_g)
        p_begin(S, "pf")
        for ts in range(4):
            inst = t.matmul(S.pF[0:8, ts * 128:(ts + 1) * 128],
                            S.gtm[:, ts * 8:(ts + 1) * 8],
                            S.ident_f[:], is_transpose=True,
                            start=(ts == 0), stop=(ts == 3))
        te_gt = S.s_te.inc(inst)
        S.gate_guard = te_gt
        S.s_te.wait(sc, te_gt)
        inst = sc.copy(S.gT_f[0:8, ucol:ucol + TTW], S.pF[0:8, 0:TTW])
        se_gt = S.s_se.inc(inst)
        p_done(S, "pf", S.s_se, se_gt)
        S.s_se.wait(v, se_gt)
        v.tensor_copy(S.gT_hi[0:8, ucol:ucol + TTW],
                      S.gT_f[0:8, ucol:ucol + TTW])
        inst = v.tensor_tensor(S.gT_lo[0:8, ucol:ucol + TTW],
                               S.gT_f[0:8, ucol:ucol + TTW],
                               S.gT_hi[0:8, ucol:ucol + TTW], ALU.subtract)
        S.gT_ready[tt] = S.s_ve.inc(inst)


def plan_ffn_fills(S, wr, tag):
    idx = {}
    for th in range(2):
        for e in range(NE):
            for m in range(4):
                for hh in range(2):
                    icc = m * 2 + hh
                    idx[(tag, th, "w1", e, icc)] = wr.queue(
                        S.colslot(S.w1t, icc * 128, e))
                    idx[(tag, th, "w3", e, icc)] = wr.queue(
                        S.colslot(S.w3t, icc * 128, e))
                idx[(tag, th, "w2", e, m)] = wr.queue(
                    S.rowslot(S.w2t, m * 256, e))
                wr.queue(S.rowslot(S.w2t, m * 256 + 128, e))
        for m in range(8):
            for hh in range(2):
                icc = m * 2 + hh
                idx[(tag, th, "ws1", icc)] = wr.queue(
                    S.colslot(S.ws1t, icc * 128))
                idx[(tag, th, "ws3", icc)] = wr.queue(
                    S.colslot(S.ws3t, icc * 128))
            idx[(tag, th, "ws2", m)] = wr.queue(S.rowslot(S.ws2t, m * 256))
            wr.queue(S.rowslot(S.ws2t, m * 256 + 128))
    return idx


def emit_h_epilogue(S, pY1, pY3, b1col, b3col, geb, h_hi_sl, h_lo_sl):
    nc = S.nc
    v, sc = nc.vector, nc.scalar
    k = S.s1_i % 2
    s1 = S.s1[k]
    if S.s1_tok[k] is not None:
        S.s_ve.wait(sc, S.s1_tok[k])
    inst = sc.activation(s1[:], pY1, AF.Silu, bias=b1col)
    se_n = S.s_se.inc(inst)
    S.s_se.wait(v, se_n)
    v.scalar_tensor_tensor(S.tmpB[:], pY3, b3col, s1[:], ALU.add, ALU.mult)
    if geb is not None:
        v.drain()
        v.tensor_mul(S.tmpA[:], S.tmpB[:], geb)
        hsrc = S.tmpA[:]
    else:
        hsrc = S.tmpB[:]
    v.tensor_copy(h_hi_sl, hsrc)
    inst = v.tensor_tensor(h_lo_sl, hsrc, h_hi_sl, ALU.subtract)
    ve_n = S.s_ve.inc(inst)
    S.s1_tok[k] = ve_n
    S.s1_i += 1
    return se_n, ve_n


def emit_ffn(S, wr, fidx, tag):
    nc = S.nc
    t, v, sc = nc.tensor, nc.vector, nc.scalar

    skip_exp = bool(os.environ.get("KSKIP_EXP"))
    skip_sh = bool(os.environ.get("KSKIP_SH"))
    for th in range(2):
        emit_layernorm(S, th)
        emit_gate(S, th)
        if S.taps and tag == "ffn1" and th == 0:
            gp = nc.gpsimd
            S.s_ve.wait(gp)
            S.s_gdma.inc(gp.dma_start(out=S.taps["tap_uh"],
                                      in_=S.u_hi.tensor.ap()), 16)
            S.s_gdma.inc(gp.dma_start(out=S.taps["tap_ul"],
                                      in_=S.u_lo.tensor.ap()), 16)
            S.s_gdma.inc(gp.dma_start(out=S.taps["tap_g"],
                                      in_=S.gT_f), 16)
            S.s_gdma.wait(gp)
        xbase = th * TH

        def macro_pass(e, m, nmac):
            ve_h_tt = [0, 0]
            for hh in range(2):
                icc = m * 2 + hh
                if e is not None:
                    sA = fidx[(tag, th, "w1", e, icc)]
                    sB = fidx[(tag, th, "w3", e, icc)]
                else:
                    sA = fidx[(tag, th, "ws1", icc)]
                    sB = fidx[(tag, th, "ws3", icc)]
                wr.acquire([sA, sB])
                w1h, w1l = wr.view(sA, DC)
                w3h, w3l = wr.view(sB, DC)
                for tt in range(NTT):
                    ucol = tt * TTW
                    S.s_ve.wait(t, S.u_ready[tt])
                    ch1 = [(w1h[:, dc, :], w1l[:, dc, :],
                            S.u_hi[:, dc, ucol:ucol + TTW],
                            S.u_lo[:, dc, ucol:ucol + TTW])
                           for dc in range(DC)]
                    ch3 = [(w3h[:, dc, :], w3l[:, dc, :],
                            S.u_hi[:, dc, ucol:ucol + TTW],
                            S.u_lo[:, dc, ucol:ucol + TTW])
                           for dc in range(DC)]
                    pY1 = S.ppAB.begin()
                    mm3_group(S, pY1[:], ch1)
                    pY3 = S.ppCD.begin()
                    i3 = mm3_group(S, pY3[:], ch3)
                    te_n = S.s_te.inc(i3)
                    S.s_te.wait(sc, te_n)
                    S.s_te.wait(v, te_n)
                    if e is not None:
                        b1col = S.b1[:, e * 8 + icc:e * 8 + icc + 1]
                        b3col = S.b3[:, e * 8 + icc:e * 8 + icc + 1]
                        geb = S.geb[:, ucol:ucol + TTW]
                    else:
                        b1col = S.bs1[:, icc:icc + 1]
                        b3col = S.bs3[:, icc:icc + 1]
                        geb = None
                    se_n, ve_n = emit_h_epilogue(
                        S, pY1[:], pY3[:], b1col, b3col, geb,
                        S.h_hi[:, hh, ucol:ucol + TTW],
                        S.h_lo[:, hh, ucol:ucol + TTW])
                    ve_h_tt[tt] = ve_n
                    S.geb_guard = ve_n
                    S.ppAB.done(S.s_se, se_n)
                    S.ppCD.done(S.s_ve, ve_n)
                wr.release([sA, sB])
            if e is not None:
                sC = fidx[(tag, th, "w2", e, m)]
            else:
                sC = fidx[(tag, th, "ws2", m)]
            wr.acquire([sC, sC + 1])
            w2h0, w2l0 = wr.view(sC, 1)
            w2h1, w2l1 = wr.view(sC + 1, 1)
            last_bias = (e is None and m == nmac - 1)
            for tt in range(NTT):
                for oc in range(DC):
                    och = slice(oc * 128, (oc + 1) * 128)
                    ucol = tt * TTW
                    xsl = S.X[:, oc, xbase + ucol:xbase + ucol + TTW]
                    S.s_ve.wait(t, ve_h_tt[tt])
                    ch = [(w2h0[:, 0, och], w2l0[:, 0, och],
                           S.h_hi[:, 0, ucol:ucol + TTW],
                           S.h_lo[:, 0, ucol:ucol + TTW]),
                          (w2h1[:, 0, och], w2l1[:, 0, och],
                           S.h_hi[:, 1, ucol:ucol + TTW],
                           S.h_lo[:, 1, ucol:ucol + TTW])]
                    if last_bias:
                        S.s_ve.wait(t, S.gT_ready[tt])
                        ch.append((S.b2_hi[0:8, och], S.b2_lo[0:8, och],
                                   S.gT_hi[0:8, ucol:ucol + TTW],
                                   S.gT_lo[0:8, ucol:ucol + TTW]))
                        ch.append((S.bs2_hi[0:1, och], S.bs2_lo[0:1, och],
                                   S.ones_row_b[0:1, 0:TTW], None))
                    pp = S.ppAB.begin()
                    inst = mm3_group(S, pp[:], ch)
                    te_n = S.s_te.inc(inst)
                    S.s_te.wait(v, te_n)
                    inst = v.scalar_tensor_tensor(xsl, pp[:], 0.5, xsl,
                                                  ALU.mult, ALU.add)
                    ve_n = S.s_ve.inc(inst)
                    S.x_guard = ve_n
                    S.ppAB.done(S.s_ve, ve_n)
            wr.release([sC, sC + 1])

        for e in range(NE):
            if skip_exp:
                for m in range(4):
                    for hh in range(2):
                        icc = m * 2 + hh
                        wr.acquire([fidx[(tag, th, "w1", e, icc)],
                                    fidx[(tag, th, "w3", e, icc)]])
                        wr.release([fidx[(tag, th, "w1", e, icc)],
                                    fidx[(tag, th, "w3", e, icc)]])
                    sC = fidx[(tag, th, "w2", e, m)]
                    wr.acquire([sC, sC + 1])
                    wr.release([sC, sC + 1])
                continue
            for tt in range(NTT):
                ucol = tt * TTW
                S.s_ve.wait(t, S.gT_ready[tt])
                p_begin(S, "pe")
                t.matmul(S.pE[:, 0:TTW],
                         S.esel[0:8, e * 128:(e + 1) * 128],
                         S.gT_hi[0:8, ucol:ucol + TTW],
                         start=True, stop=False)
                inst = t.matmul(S.pE[:, 0:TTW],
                                S.esel[0:8, e * 128:(e + 1) * 128],
                                S.gT_lo[0:8, ucol:ucol + TTW],
                                start=False, stop=True)
                te_n = S.s_te.inc(inst)
                S.s_ve.wait(sc, S.geb_guard)
                S.s_te.wait(sc, te_n)
                inst = sc.copy(S.geb[:, ucol:ucol + TTW], S.pE[:, 0:TTW])
                se_n = S.s_se.inc(inst)
                p_done(S, "pe", S.s_se, se_n)
            for m in range(4):
                macro_pass(e, m, 4)
                if S.taps and tag == "ffn1" and th == 0 and e == 1 \
                        and m == 0 and os.environ.get("KTAP3"):
                    gp = nc.gpsimd
                    S.s_ve.wait(gp)
                    S.s_se.wait(gp)
                    S.s_gdma.inc(gp.dma_start(out=S.taps["tap_geb"],
                                              in_=S.geb), 16)
                    S.s_gdma.inc(gp.dma_start(out=S.taps["tap_hh"],
                                              in_=S.h_hi.tensor.ap()), 16)
                    S.s_gdma.inc(gp.dma_start(out=S.taps["tap_hl"],
                                              in_=S.h_lo.tensor.ap()), 16)
                    S.s_gdma.wait(gp)
            if S.taps and tag == "ffn1" and th == 0 and e == 0 \
                    and os.environ.get("KTAP2"):
                gp = nc.gpsimd
                S.s_ve.wait(gp)
                S.s_gdma.inc(gp.dma_start(out=S.taps["tap_x0"],
                                          in_=S.X.tensor.ap()), 16)
                S.s_gdma.wait(gp)
        for m in range(8):
            if skip_sh:
                for hh in range(2):
                    icc = m * 2 + hh
                    wr.acquire([fidx[(tag, th, "ws1", icc)],
                                fidx[(tag, th, "ws3", icc)]])
                    wr.release([fidx[(tag, th, "ws1", icc)],
                                fidx[(tag, th, "ws3", icc)]])
                sC = fidx[(tag, th, "ws2", m)]
                wr.acquire([sC, sC + 1])
                wr.release([sC, sC + 1])
                continue
            macro_pass(None, m, 8)


def plan_attn_fills(S, wr):
    idx = {}
    for th in range(2):
        for w, wt in (("wk", S.wkt), ("wv", S.wvt), ("wq", S.wqt)):
            for cv in range(DC):
                idx[("attn", th, w, cv)] = wr.queue(S.colslot(wt, cv * 128))
    return idx


def kv_slot(S, eng):
    k = S.kv_i % 2
    tok = S.kv_tok[k]
    if tok is not None:
        tok[0].wait(eng, tok[1])
    return k


def kv_mark(S, k, sem):
    S.kv_tok[k] = (sem, sem.n)
    S.kv_i += 1


def emit_qkv(S, wr, fidx):
    nc = S.nc
    t, v, sc, g, sy = nc.tensor, nc.vector, nc.scalar, nc.gpsimd, nc.sync
    tload_i = [0]

    def load_t(which_hi, which_lo, dcg, gw0):
        k = tload_i[0] % 2
        if S.t_tok[k] is not None:
            S.s_ve.wait(sy, S.t_tok[k])
        S.s_tdma.inc(sy.dma_start(
            out=S.tbh[k][:], in_=which_hi[dcg * 128:(dcg + 1) * 128,
                                          gw0:gw0 + TTW]), 16)
        n = S.s_tdma.inc(sy.dma_start(
            out=S.tbl[k][:], in_=which_lo[dcg * 128:(dcg + 1) * 128,
                                          gw0:gw0 + TTW]), 16)
        tload_i[0] += 1
        return S.tbh[k], S.tbl[k], k, n

    for th in range(2):
        emit_layernorm(S, th)
        for proj in ("wk", "wv", "wq"):
            for cv in range(DC):
                sA = fidx[("attn", th, proj, cv)]
                wr.acquire([sA])
                wh, wl = wr.view(sA, DC)
                dcg = cv
                for tt in range(NTT):
                    ucol = tt * TTW
                    gw0 = th * TH + ucol
                    S.s_ve.wait(t, S.u_ready[tt])
                    ch = [(wh[:, dc, :], wl[:, dc, :],
                           S.u_hi[:, dc, ucol:ucol + TTW],
                           S.u_lo[:, dc, ucol:ucol + TTW])
                          for dc in range(DC)]
                    pp = S.ppAB.begin()
                    inst = mm3_group(S, pp[:], ch)
                    te_p = S.s_te.inc(inst)
                    if proj == "wv":
                        bcol = S.bv[:, dcg:dcg + 1]
                        if S.vtr_tok:
                            S.s_te.wait(sc, S.vtr_tok)
                        S.s_te.wait(sc, te_p)
                        inst = sc.activation(S.vtmp[:], pp[:], AF.Identity,
                                             bias=bcol)
                        se_v = S.s_se.inc(inst)
                        S.ppAB.done(S.s_se, se_v)
                        S.s_se.wait(t, se_v)
                        p_begin(S, "pf")
                        for tb in range(4):
                            inst = t.matmul(
                                S.pF[:, tb * 128:(tb + 1) * 128],
                                S.vtmp[:, tb * 128:(tb + 1) * 128],
                                S.ident_f[:], is_transpose=True,
                                start=(tb == 0), stop=(tb == 3))
                        te_tr = S.s_te.inc(inst)
                        S.vtr_tok = te_tr
                        S.s_te.wait(v, te_tr)
                        k = kv_slot(S, v)
                        v.tensor_copy(S.kvh[k][:], S.pF[:])
                        inst = v.tensor_tensor(S.kvl[k][:], S.pF[:],
                                               S.kvh[k][:], ALU.subtract)
                        ve_v = S.s_ve.inc(inst)
                        p_done(S, "pf", S.s_ve, ve_v)
                        S.s_ve.wait(g, ve_v)
                        dst = []
                        for base in (0, 2048):
                            r0 = base + gw0
                            dst.append((
                                S.v_sh_hi[dcg, r0:r0 + TTW, :].rearrange(
                                    "(a p) f -> p a f", p=128),
                                S.v_sh_lo[dcg, r0:r0 + TTW, :].rearrange(
                                    "(a p) f -> p a f", p=128)))
                        srch = S.kvh[k][:].rearrange("p (a b) -> p a b", a=4)
                        srcl = S.kvl[k][:].rearrange("p (a b) -> p a b", a=4)
                        with g.If(S.par == 0):
                            g.dma_start(out=dst[0][0], in_=srch) \
                                .then_inc(S.s_gkv.h, 16)
                            g.dma_start(out=dst[0][1], in_=srcl) \
                                .then_inc(S.s_gkv.h, 16)
                        with g.Else():
                            g.dma_start(out=dst[1][0], in_=srch) \
                                .then_inc(S.s_gkv.h, 16)
                            g.dma_start(out=dst[1][1], in_=srcl) \
                                .then_inc(S.s_gkv.h, 16)
                        S.s_gkv.n += 32
                        kv_mark(S, k, S.s_gkv)
                        continue
                    is_q = proj == "wq"
                    bcol = (S.bq if is_q else S.bk)[:, dcg:dcg + 1]
                    kq = S.qk_i % 2
                    if S.qk_tok[kq] is not None:
                        S.s_te.wait(sc, S.qk_tok[kq])
                        S.s_te.wait(v, S.qk_tok[kq])
                    S.s_te.wait(sc, te_p)
                    inst = sc.activation(S.qh_t[kq][:], pp[:], AF.Identity,
                                         bias=bcol)
                    se_q = S.s_se.inc(inst)
                    S.s_te.wait(v, te_p)
                    S.s_se.wait(v, se_q)
                    inst = v.scalar_tensor_tensor(
                        S.ql_t[kq][:], pp[:], bcol, S.qh_t[kq][:],
                        ALU.add, ALU.subtract)
                    ve_q = S.s_ve.inc(inst)
                    S.ppAB.done(S.s_ve, ve_q)
                    S.s_se.wait(t, se_q)
                    S.s_ve.wait(t, ve_q)
                    p_begin(S, "pe")
                    t.matmul(S.pE[:, 0:TTW], S.ropeAT[:], S.qh_t[kq][:],
                             start=True, stop=False)
                    inst = t.matmul(S.pE[:, 0:TTW], S.ropeAT[:],
                                    S.ql_t[kq][:], start=False, stop=True)
                    te_r = S.s_te.inc(inst)
                    S.qk_tok[kq] = te_r
                    S.qk_i += 1
                    tbh, tbl, tk, tdma_n = load_t(
                        S.tq_hi if is_q else S.tk_hi,
                        S.tq_lo if is_q else S.tk_lo, dcg, gw0)
                    S.s_tdma.wait(v, tdma_n)
                    S.s_te.wait(v, te_r)
                    v.tensor_mul(S.tmpA[:], S.pE[:, 0:TTW], tbh[:])
                    v.tensor_mul(S.tmpB[:], S.pE[:, 0:TTW], tbl[:])
                    v.tensor_add(S.tmpA[:], S.tmpA[:], S.tmpB[:])
                    k2 = kv_slot(S, v)
                    v.tensor_copy(S.kvh[k2][:], S.tmpA[:])
                    inst = v.tensor_tensor(S.kvl[k2][:], S.tmpA[:],
                                           S.kvh[k2][:], ALU.subtract)
                    ve_s = S.s_ve.inc(inst)
                    S.t_tok[tk] = ve_s
                    p_done(S, "pe", S.s_ve, ve_s)
                    qb = th * NTT + tt
                    S.s_ve.wait(g, ve_s)
                    if is_q:
                        g.dma_start(
                            out=S.q_sp_hi[qb, dcg * 128:(dcg + 1) * 128, :],
                            in_=S.kvh[k2][:]).then_inc(S.s_gq.h, 16)
                        g.dma_start(
                            out=S.q_sp_lo[qb, dcg * 128:(dcg + 1) * 128, :],
                            in_=S.kvl[k2][:]).then_inc(S.s_gq.h, 16)
                        S.s_gq.n += 32
                        kv_mark(S, k2, S.s_gq)
                        S.qspill_done[qb] = S.s_gq.n
                    else:
                        r = slice(dcg * 128, (dcg + 1) * 128)
                        with g.If(S.par == 0):
                            g.dma_start(out=S.k_sh_hi[r, gw0:gw0 + TTW],
                                        in_=S.kvh[k2][:]).then_inc(
                                S.s_gkv.h, 16)
                            g.dma_start(out=S.k_sh_lo[r, gw0:gw0 + TTW],
                                        in_=S.kvl[k2][:]).then_inc(
                                S.s_gkv.h, 16)
                        with g.Else():
                            g.dma_start(
                                out=S.k_sh_hi[r, 2048 + gw0:2048 + gw0 + TTW],
                                in_=S.kvh[k2][:]).then_inc(S.s_gkv.h, 16)
                            g.dma_start(
                                out=S.k_sh_lo[r, 2048 + gw0:2048 + gw0 + TTW],
                                in_=S.kvl[k2][:]).then_inc(S.s_gkv.h, 16)
                        S.s_gkv.n += 32
                        kv_mark(S, k2, S.s_gkv)
                wr.release([sA])


def emit_scores(S):
    nc = S.nc
    t, v, sc, g, sy = nc.tensor, nc.vector, nc.scalar, nc.gpsimd, nc.sync
    LVL = int(os.environ.get("KSCORE_LVL", "5"))
    S.s_gkv.wait(g)
    S.s_gq.wait(g)
    inst = g.collective_compute(
        "AllGather", ALU.bypass, [[0, 1], [2, 3], [4, 5], [6, 7]],
        ins=[S.bar_in], outs=[S.bar_out])
    S.s_bar.inc(inst)

    kst_tok = [0]
    vr_tok = [None, None, None]

    def load_kstage(ck):
        S.s_bar.wait(sy, 1)
        if kst_tok[0]:
            S.s_te.wait(sy, kst_tok[0])
        src_h = S.k_sh_hi[:, ck * 512:(ck + 1) * 512].rearrange(
            "(a p) n -> p a n", p=128)
        src_l = S.k_sh_lo[:, ck * 512:(ck + 1) * 512].rearrange(
            "(a p) n -> p a n", p=128)
        dst_h = S.ksth[:].rearrange("p (a n) -> p a n", a=DC)
        dst_l = S.kstl[:].rearrange("p (a n) -> p a n", a=DC)
        S.s_kdma.inc(sy.dma_start(out=dst_h, in_=src_h), 16)
        return S.s_kdma.inc(sy.dma_start(out=dst_l, in_=src_l), 16)

    def load_vblock(ck, oc, ring_i):
        k = ring_i % 3
        S.s_bar.wait(sy, 1)
        if vr_tok[k] is not None:
            S.s_te.wait(sy, vr_tok[k])
        src_h = S.v_sh_hi[oc, ck * 512:(ck + 1) * 512, :].rearrange(
            "(a p) f -> p a f", p=128)
        src_l = S.v_sh_lo[oc, ck * 512:(ck + 1) * 512, :].rearrange(
            "(a p) f -> p a f", p=128)
        dst_h = S.vrh[k][:].rearrange("p (a f) -> p a f", a=4)
        dst_l = S.vrl[k][:].rearrange("p (a f) -> p a f", a=4)
        S.s_vdma.inc(sy.dma_start(out=dst_h, in_=src_h), 16)
        n = S.s_vdma.inc(sy.dma_start(out=dst_l, in_=src_l), 16)
        return k, n

    qfetch_n = {}

    def fetch_q(qb):
        S.s_bar.wait(sy, 1)
        S.s_gq.wait(sy, S.qspill_done[qb])
        src_h = S.q_sp_hi[qb].rearrange("(a p) n -> p a n", p=128)
        src_l = S.q_sp_lo[qb].rearrange("(a p) n -> p a n", p=128)
        dst_h = S.qsh[:].rearrange("p (a n) -> p a n", a=DC)
        dst_l = S.qsl[:].rearrange("p (a n) -> p a n", a=DC)
        S.s_qdma.inc(sy.dma_start(out=dst_h, in_=src_h), 16)
        qfetch_n[qb] = S.s_qdma.inc(
            sy.dma_start(out=dst_l, in_=src_l), 16)

    ck_load_n = {}
    vring_i = [0]
    if LVL == 0:
        return
    for qb in range(int(os.environ.get("KQB", "4"))):
        if qb > 0:
            S.s_te.wait(sy)
        fetch_q(qb)
        if qb == 0:
            ck_load_n[0] = load_kstage(0)
        qsh = S.qsh[:].rearrange("p (a n) -> p a n", a=DC)
        qsl = S.qsl[:].rearrange("p (a n) -> p a n", a=DC)
        ksth = S.ksth[:].rearrange("p (a n) -> p a n", a=DC)
        kstl = S.kstl[:].rearrange("p (a n) -> p a n", a=DC)
        p_begin(S, "pS")
        denom_first = [True]
        denom_last = [None]
        if LVL == 1:
            for ck in range(1, 8):
                ck_load_n[ck] = load_kstage(ck)
            continue
        for ck in range(8):
            S.s_kdma.wait(t, ck_load_n[ck])
            S.s_qdma.wait(t, qfetch_n[qb])
            for kb in range(4):
                ko = slice(kb * 128, (kb + 1) * 128)
                ch = [(ksth[:, dc, ko], kstl[:, dc, ko],
                       qsh[:, dc, :], qsl[:, dc, :]) for dc in range(DC)]
                pp = S.ppAB.begin()
                inst = mm3_group(S, pp[:], ch)
                te_s = S.s_te.inc(inst)
                S.s_te.wait(sc, te_s)
                kpp = S.pt_i % 2
                if S.pt_tok[kpp] is not None:
                    S.s_ve.wait(sc, S.pt_tok[kpp])
                if S.den_tok[kpp] is not None:
                    S.s_te.wait(sc, S.den_tok[kpp])
                inst = sc.activation(S.ptmp[kpp][:], pp[:], AF.Exp)
                se_e = S.s_se.inc(inst)
                S.ppAB.done(S.s_se, se_e)
                S.s_se.wait(v, se_e)
                v.tensor_copy(S.pT_hi[:, kb, :], S.ptmp[kpp][:])
                inst = v.tensor_tensor(S.pT_lo[:, kb, :], S.ptmp[kpp][:],
                                       S.pT_hi[:, kb, :], ALU.subtract)
                ve_p = S.s_ve.inc(inst)
                S.pt_tok[kpp] = ve_p
                S.pt_i += 1
                if LVL >= 3:
                    S.s_se.wait(t, se_e)
                    denom_last[0] = t.matmul(
                        S.pS[0:1, 0:TTW], S.ones_col_f[:, 0:1],
                        S.ptmp[kpp][:], start=denom_first[0],
                        stop=(ck == 7 and kb == 3))
                    denom_first[0] = False
                    S.den_tok[kpp] = S.s_te.inc(denom_last[0])
            ve_p_last = ve_p
            if LVL >= 4:
                vloads = {}
                vloads[0] = load_vblock(ck, 0, vring_i[0]); vring_i[0] += 1
                vloads[1] = load_vblock(ck, 1, vring_i[0]); vring_i[0] += 1
            kst_tok[0] = S.s_te.n
            if ck < 7:
                ck_load_n[ck + 1] = load_kstage(ck + 1)
            elif qb < 3:
                ck_load_n[0] = load_kstage(0)
            if LVL < 4:
                continue
            for oc in range(DC):
                if oc + 2 < DC:
                    vloads[oc + 2] = load_vblock(ck, oc + 2, vring_i[0])
                    vring_i[0] += 1
                vk, vn = vloads[oc]
                S.s_ve.wait(t, ve_p_last)
                S.s_vdma.wait(t, vn)
                vh = S.vrh[vk][:].rearrange("p (a f) -> p a f", a=4)
                vl = S.vrl[vk][:].rearrange("p (a f) -> p a f", a=4)
                ch = [(vh[:, a, :], vl[:, a, :], S.pT_hi[:, a, :],
                       S.pT_lo[:, a, :]) for a in range(4)]
                pp = S.ppCD.begin()
                inst = mm3_group(S, pp[:], ch)
                te_pv = S.s_te.inc(inst)
                vr_tok[vk] = te_pv
                if ck == 0:
                    if S.oacc_guard:
                        S.s_ve.wait(sc, S.oacc_guard)
                    S.s_te.wait(sc, te_pv)
                    inst = sc.copy(S.oacc[:, oc, :], pp[:])
                    n = S.s_se.inc(inst)
                    S.ppCD.done(S.s_se, n)
                else:
                    S.s_te.wait(v, te_pv)
                    inst = v.tensor_add(S.oacc[:, oc, :], S.oacc[:, oc, :],
                                        pp[:])
                    n = S.s_ve.inc(inst)
                    S.ppCD.done(S.s_ve, n)
        if LVL < 5:
            if denom_last[0] is not None and (LVL >= 3):
                te_d = S.s_te.inc(denom_last[0])
                p_done(S, "pS", S.s_te, te_d)
            continue
        KTAIL = int(os.environ.get("KTAIL", "3"))
        te_den = S.den_tok[(S.pt_i - 1) % 2]
        S.s_te.wait(v, te_den)
        v.tensor_copy(S.sm[0:1, 0:TTW], S.pS[0:1, 0:TTW])
        v.drain()
        inst = v.reciprocal(S.sv[0:1, 0:TTW], S.sm[0:1, 0:TTW])
        ve_r = S.s_ve.inc(inst)
        p_done(S, "pS", S.s_ve, ve_r)
        if KTAIL < 2:
            S.oacc_guard = 0
            continue
        S.s_ve.wait(t, ve_r)
        p_begin(S, "pe")
        inst = t.matmul(S.pE[:, 0:TTW], S.ones_row_f[0:1, :],
                        S.sv[0:1, 0:TTW], start=True, stop=True)
        te_b = S.s_te.inc(inst)
        if KTAIL < 3:
            p_done(S, "pe", S.s_te, te_b)
            S.oacc_guard = 0
            continue
        S.s_te.wait(v, te_b)
        qcol = qb * TTW
        inst = None
        for oc in range(DC):
            v.tensor_mul(S.tmpA[:], S.oacc[:, oc, :], S.pE[:, 0:TTW])
            inst = v.tensor_add(S.X[:, oc, qcol:qcol + TTW],
                                S.X[:, oc, qcol:qcol + TTW], S.tmpA[:])
        ve_x = S.s_ve.inc(inst)
        S.x_guard = ve_x
        p_done(S, "pe", S.s_ve, ve_x)
        S.oacc_guard = ve_x


def build_program(stage=5):
    nc = bass.Bass("TRN2", target_bir_lowering=False)
    S = SimpleNamespace()
    S.nc = nc

    def din(name, shape, dt=F32):
        return nc.dram_tensor(name, shape, dt, kind="ExternalInput").ap()

    S.x_in = din("x_in", [D, TPC])
    S.taps = {}
    if os.environ.get("KTAP"):
        for nm, shape, dt in [("tap_uh", [128, DC * TH], BF16),
                              ("tap_ul", [128, DC * TH], BF16),
                              ("tap_g", [NE, TH], F32),
                              ("tap_x0", [128, DC * TPC], F32),
                              ("tap_geb", [128, TH], F32),
                              ("tap_hh", [128, 2 * TH], BF16),
                              ("tap_hl", [128, 2 * TH], BF16)]:
            S.taps[nm] = nc.dram_tensor(nm, shape, dt,
                                        kind="ExternalOutput").ap()
    S.y_out = nc.dram_tensor("y_out", [D, TPC], F32,
                             kind="ExternalOutput").ap()
    S.wqt = din("wqt", [D, D])
    S.wkt = din("wkt", [D, D])
    S.wvt = din("wvt", [D, D])
    S.w1t = din("w1t", [NE, D, D])
    S.w3t = din("w3t", [NE, D, D])
    S.w2t = din("w2t", [NE, D, D])
    S.ws1t = din("ws1t", [D, SINT])
    S.ws3t = din("ws3t", [D, SINT])
    S.ws2t = din("ws2t", [SINT, D])
    S.gwt = din("gwt", [D, NE])
    S.bq_in = din("bq_in", [128, DC])
    S.bk_in = din("bk_in", [128, DC])
    S.bv_in = din("bv_in", [128, DC])
    S.b1_in = din("b1_in", [128, 64])
    S.b3_in = din("b3_in", [128, 64])
    S.b2_in = din("b2_in", [NE, D])
    S.bs1_in = din("bs1_in", [128, 16])
    S.bs3_in = din("bs3_in", [128, 16])
    S.bs2_in = din("bs2_in", [1, D])
    S.gb_in = din("gb_in", [NE, 1])
    S.tq_hi = din("tq_hi", [D, TPC])
    S.tq_lo = din("tq_lo", [D, TPC])
    S.tk_hi = din("tk_hi", [D, TPC])
    S.tk_lo = din("tk_lo", [D, TPC])
    S.ropeAT_in = din("ropeAT_in", [128, 128], BF16)
    S.ident_in = din("ident_in", [128, 128])
    S.esel_in = din("esel_in", [NE, NE * 128], BF16)

    S.q_sp_hi = nc.dram_tensor("q_sp_hi", [4, D, TTW], BF16).ap()
    S.q_sp_lo = nc.dram_tensor("q_sp_lo", [4, D, TTW], BF16).ap()
    S.k_sh_hi = nc.dram_tensor("k_sh_hi", [D, SEQ], BF16,
                               addr_space="Shared").ap()
    S.k_sh_lo = nc.dram_tensor("k_sh_lo", [D, SEQ], BF16,
                               addr_space="Shared").ap()
    S.v_sh_hi = nc.dram_tensor("v_sh_hi", [DC, SEQ, 128], BF16,
                               addr_space="Shared").ap()
    S.v_sh_lo = nc.dram_tensor("v_sh_lo", [DC, SEQ, 128], BF16,
                               addr_space="Shared").ap()
    S.bar_in = nc.dram_tensor("bar_in", [1, 32], F32).ap()
    S.bar_out = nc.dram_tensor("bar_out", [2, 32], F32).ap()

    def colslot(wt, c0, e=None):
        ap = wt if e is None else wt[e]
        return ap[:, c0:c0 + 128].rearrange("(a p) n -> p a n", p=128)

    def rowslot(wt, r0, e=None):
        ap = wt if e is None else wt[e]
        return ap[r0:r0 + 128, :].rearrange("(a p) n -> p a n", p=128)

    S.colslot = colslot
    S.rowslot = rowslot

    with ExitStack() as ctx:
        S.ctx = ctx

        def sb(name, shape, dt=F32, c=None):
            return (c or ctx).enter_context(
                nc.sbuf_tensor(name, shape, dt))[:]

        S.X = sb("X", [128, DC * TPC]).rearrange("p (a n) -> p a n", a=DC)
        S.tmpA = sb("tmpA", [128, TTW])
        S.tmpB = sb("tmpB", [128, TTW])
        S.sq = [sb(f"sq{i}", [128, TTW]) for i in range(2)]
        S.s1 = [sb(f"s1_{i}", [128, TTW]) for i in range(2)]
        S.stat = sb("stat", [1, 2048])
        S.sm = sb("sm", [1, TTW])
        S.sv = sb("sv", [1, TTW])
        S.sw = sb("sw", [1, TTW])
        S.si = sb("si", [1, TTW])
        S.glog = sb("glog", [NE, TTW])
        S.ge = sb("gexp", [128, 32])
        S.gtm = sb("gtm", [128, 32])
        S.g_l = sb("g_l", [128, 4])
        S.g_m1 = sb("g_m1", [128, 4])
        S.g_rl = sb("g_rl", [128, 4])
        S.g_m2 = sb("g_m2", [128, 1])
        S.g_eq = sb("g_eq", [128, 8])
        S.g_wo = sb("g_wo", [128, 8])
        S.g_gu = sb("g_gu", [128, 8])
        S.zeros8 = sb("zeros8", [128, 8])
        S.gT_f = sb("gT_f", [NE, TH])
        S.gT_hi = sb("gT_hi", [NE, TH], BF16)
        S.gT_lo = sb("gT_lo", [NE, TH], BF16)
        S.gw_hi = sb("gw_hi", [128, DC * 8], BF16).rearrange(
            "p (a n) -> p a n", a=DC)
        S.gw_lo = sb("gw_lo", [128, DC * 8], BF16).rearrange(
            "p (a n) -> p a n", a=DC)
        S.bq = sb("bq", [128, DC])
        S.bk = sb("bk", [128, DC])
        S.bv = sb("bv", [128, DC])
        S.b1 = sb("b1", [128, 64])
        S.b3 = sb("b3", [128, 64])
        S.bs1 = sb("bs1", [128, 16])
        S.bs3 = sb("bs3", [128, 16])
        S.gb = sb("gb", [NE, 1])
        S.b2_hi = sb("b2_hi", [NE, D], BF16)
        S.b2_lo = sb("b2_lo", [NE, D], BF16)
        S.bs2_hi = sb("bs2_hi", [1, D], BF16)
        S.bs2_lo = sb("bs2_lo", [1, D], BF16)
        S.ones_col_f = sb("ones_col_f", [128, 1])
        S.ones_col_b = sb("ones_col_b", [128, 1], BF16)
        S.ones_row_f = sb("ones_row_f", [1, 128])
        S.ones_row_b = sb("ones_row_b", [1, TTW], BF16)
        S.ropeAT = sb("ropeAT", [128, 128], BF16)
        S.ident_f = sb("ident_f", [128, 128])
        S.esel = sb("esel", [NE, NE * 128], BF16)

        S.banks = {}
        for nm in ("pa", "pb", "pc", "pd", "pe", "pf"):
            S.banks[nm] = ctx.enter_context(
                nc.psum_tensor(nm, [128, TTW], F32))[:]
        S.pE = S.banks["pe"]
        S.pF = S.banks["pf"]
        S.pS = ctx.enter_context(nc.psum_tensor("ps", [1, 1024], F32))[:]

        for nm in ("s_wdma", "s_tdma", "s_kdma", "s_vdma", "s_qdma",
                   "s_gdma", "s_gkv", "s_gq", "s_te", "s_ve", "s_se",
                   "s_bar"):
            setattr(S, nm, Sem(ctx, nc, nm))

        S.ptok = {}
        S.ppAB = PsumRing(S, ["pa", "pb", "pe"])
        S.ppCD = PsumRing(S, ["pc", "pd", "pf"])
        S.u_ready = [0, 0]
        S.gT_ready = [0, 0]
        S.s1_i = 0
        S.s1_tok = [None, None]
        S.qk_i = 0
        S.qk_tok = [None, None]
        S.kv_i = 0
        S.kv_tok = [None, None]
        S.t_tok = [None, None]
        S.pt_i = 0
        S.pt_tok = [None, None]
        S.den_tok = [None, None]
        S.qspill_done = {}
        S.sq_guard = 0
        S.gate_guard = 0
        S.vtr_tok = 0
        S.oacc_guard = 0
        S.x_guard = 0
        S.lnstat_guard = 0
        S.ln_bc_guard = 0
        S.geb_guard = 0
        S.gate_guard_ve = 0

        with nc.Block():
            g, v = nc.gpsimd, nc.vector
            pid = g.partition_id()
            S.par = g.compute_val(pid % 2)
            with ExitStack() as c0:
                b2f = sb("b2_f", [NE, D], c=c0)
                bs2f = sb("bs2_f", [1, D], c=c0)
                gwstage = sb("gwstage", [128, DC * 8], c=c0)
                for dst, src in [
                    (S.X, S.x_in.rearrange("(a p) n -> p a n", p=128)),
                    (gwstage.rearrange("p (a n) -> p a n", a=DC),
                     S.gwt.rearrange("(a p) n -> p a n", p=128)),
                    (S.bq, S.bq_in), (S.bk, S.bk_in), (S.bv, S.bv_in),
                    (S.b1, S.b1_in), (S.b3, S.b3_in),
                    (S.bs1, S.bs1_in), (S.bs3, S.bs3_in),
                    (S.gb, S.gb_in), (b2f, S.b2_in), (bs2f, S.bs2_in),
                    (S.ropeAT, S.ropeAT_in), (S.ident_f, S.ident_in),
                    (S.esel, S.esel_in),
                ]:
                    S.s_gdma.inc(g.dma_start(out=dst, in_=src), 16)
                v.memset(S.ones_col_f, 1.0)
                v.memset(S.ones_col_b, 1.0)
                v.memset(S.ones_row_f, 1.0)
                v.memset(S.ones_row_b, 1.0)
                v.memset(S.zeros8, 0.0)
                S.s_gdma.wait(v)
                v.tensor_copy(S.gw_hi.tensor.ap(), gwstage)
                v.tensor_tensor(S.gw_lo.tensor.ap(), gwstage,
                                S.gw_hi.tensor.ap(), ALU.subtract)
                v.tensor_copy(S.b2_hi, b2f)
                v.tensor_tensor(S.b2_lo, b2f, S.b2_hi, ALU.subtract)
                v.tensor_copy(S.bs2_hi, bs2f)
                inst = v.tensor_tensor(S.bs2_lo, bs2f, S.bs2_hi,
                                       ALU.subtract)
                ve0 = S.s_ve.inc(inst)
                S.s_ve.wait(nc.tensor, ve0)
                S.s_gdma.wait(nc.tensor)
                S.s_gdma.wait(nc.scalar)
                S.s_gdma.wait(nc.sync)
            wr = WRing(S)
            f1 = plan_ffn_fills(S, wr, "ffn1") if stage >= 2 else None
            fa = plan_attn_fills(S, wr) if stage >= 3 else None
            f2 = plan_ffn_fills(S, wr, "ffn2") if stage >= 5 else None

            with ExitStack() as c1:
                S.u_hi = sb("u_hi1", [128, DC * TH], BF16, c=c1).rearrange(
                    "p (a n) -> p a n", a=DC)
                S.u_lo = sb("u_lo1", [128, DC * TH], BF16, c=c1).rearrange(
                    "p (a n) -> p a n", a=DC)
                S.h_hi = sb("h_hi1", [128, 2 * TH], BF16, c=c1).rearrange(
                    "p (a n) -> p a n", a=2)
                S.h_lo = sb("h_lo1", [128, 2 * TH], BF16, c=c1).rearrange(
                    "p (a n) -> p a n", a=2)
                S.geb = sb("geb1", [128, TH], c=c1)
                S.vtmp = sb("vtmp", [128, TTW], c=c1)
                S.qh_t = [sb(f"qh_t{i}", [128, TTW], BF16, c=c1)
                          for i in range(2)]
                S.ql_t = [sb(f"ql_t{i}", [128, TTW], BF16, c=c1)
                          for i in range(2)]
                S.kvh = [sb(f"kvh{i}", [128, TTW], BF16, c=c1)
                         for i in range(2)]
                S.kvl = [sb(f"kvl{i}", [128, TTW], BF16, c=c1)
                         for i in range(2)]
                S.tbh = [sb(f"tbh{i}", [128, TTW], c=c1) for i in range(2)]
                S.tbl = [sb(f"tbl{i}", [128, TTW], c=c1) for i in range(2)]
                if stage >= 2:
                    emit_ffn(S, wr, f1, "ffn1")
                if stage >= 3:
                    emit_qkv(S, wr, fa)

            with ExitStack() as c2:
                S.qsh = sb("qsh", [128, DC * TTW], BF16, c=c2)
                S.qsl = sb("qsl", [128, DC * TTW], BF16, c=c2)
                S.ksth = sb("ksth", [128, DC * TTW], BF16, c=c2)
                S.kstl = sb("kstl", [128, DC * TTW], BF16, c=c2)
                S.vrh = [sb(f"vrh{i}", [128, TTW], BF16, c=c2)
                         for i in range(3)]
                S.vrl = [sb(f"vrl{i}", [128, TTW], BF16, c=c2)
                         for i in range(3)]
                S.pT_hi = sb("pT_hi", [128, 4 * TTW], BF16, c=c2).rearrange(
                    "p (a n) -> p a n", a=4)
                S.pT_lo = sb("pT_lo", [128, 4 * TTW], BF16, c=c2).rearrange(
                    "p (a n) -> p a n", a=4)
                S.ptmp = [sb(f"ptmp{i}", [128, TTW], c=c2) for i in range(2)]
                S.oacc = sb("oacc", [128, DC * TTW], c=c2).rearrange(
                    "p (a n) -> p a n", a=DC)
                if stage >= 4:
                    emit_scores(S)

            with ExitStack() as c3:
                S.u_hi = sb("u_hi2", [128, DC * TH], BF16, c=c3).rearrange(
                    "p (a n) -> p a n", a=DC)
                S.u_lo = sb("u_lo2", [128, DC * TH], BF16, c=c3).rearrange(
                    "p (a n) -> p a n", a=DC)
                S.h_hi = sb("h_hi2", [128, 2 * TH], BF16, c=c3).rearrange(
                    "p (a n) -> p a n", a=2)
                S.h_lo = sb("h_lo2", [128, 2 * TH], BF16, c=c3).rearrange(
                    "p (a n) -> p a n", a=2)
                S.geb = sb("geb2", [128, TH], c=c3)
                if stage >= 5:
                    emit_ffn(S, wr, f2, "ffn2")

            S.s_ve.wait(g)
            S.s_te.wait(g)
            S.s_se.wait(g)
            out_ap = S.y_out.rearrange("(a p) n -> p a n", p=128)
            S.s_gdma.inc(g.dma_start(out=out_ap, in_=S.X), 16)
            S.s_gdma.wait(g)
    return nc


def _rope_tables():
    pos = np.arange(SEQ, dtype=np.float64)
    inv = 10000.0 ** (np.arange(0, 512, 2, dtype=np.float64) / 512.0)
    radians = pos[:, None] / inv
    sincos = np.stack((np.sin(radians), np.cos(radians)),
                      axis=-1).reshape(SEQ, 512)
    return np.repeat(sincos, 2, axis=-1)


def kernel(**inputs):
    ix = {k: np.asarray(v) for k, v in inputs.items()}
    x = np.ascontiguousarray(ix["L0_attn_hidden_states"].reshape(-1, D))
    t64 = _rope_tables()
    tq64 = t64 / math.sqrt(512.0)
    tq_hi = tq64.astype(np.float32)
    tq_lo = (tq64 - tq_hi.astype(np.float64)).astype(np.float32)
    tk_hi = t64.astype(np.float32)
    tk_lo = (t64 - tk_hi.astype(np.float64)).astype(np.float32)

    def T(a):
        return np.ascontiguousarray(np.asarray(a).T)

    ropeA = np.zeros((128, 128), np.float32)
    for i in range(64):
        ropeA[2 * i, 2 * i] = 1.0
        ropeA[2 * i, 2 * i + 1] = -1.0
        ropeA[2 * i + 1, 2 * i] = 1.0
        ropeA[2 * i + 1, 2 * i + 1] = 1.0
    ropeAT = np.ascontiguousarray(ropeA.T).astype(ml_dtypes.bfloat16)

    shared = {
        "wqt": T(ix["L0_attn_Wq"]), "wkt": T(ix["L0_attn_Wk"]),
        "wvt": T(ix["L0_attn_Wv"]),
        "w1t": np.ascontiguousarray(np.transpose(ix["L0_ffn_w1"], (0, 2, 1))),
        "w3t": np.ascontiguousarray(np.transpose(ix["L0_ffn_w3"], (0, 2, 1))),
        "w2t": np.ascontiguousarray(np.transpose(ix["L0_ffn_w2"], (0, 2, 1))),
        "ws1t": T(ix["L0_ffn_ws1"]), "ws3t": T(ix["L0_ffn_ws3"]),
        "ws2t": T(ix["L0_ffn_ws2"]),
        "gwt": T(ix["L0_ffn_gate_w"]),
        "bq_in": np.ascontiguousarray(ix["L0_attn_bq"].reshape(DC, 128).T),
        "bk_in": np.ascontiguousarray(ix["L0_attn_bk"].reshape(DC, 128).T),
        "bv_in": np.ascontiguousarray(ix["L0_attn_bv"].reshape(DC, 128).T),
        "b1_in": np.ascontiguousarray(
            ix["L0_ffn_b1"].reshape(NE, DC, 128).transpose(2, 0, 1)
            .reshape(128, 64)),
        "b3_in": np.ascontiguousarray(
            ix["L0_ffn_b3"].reshape(NE, DC, 128).transpose(2, 0, 1)
            .reshape(128, 64)),
        "b2_in": np.ascontiguousarray(ix["L0_ffn_b2"]),
        "bs1_in": np.ascontiguousarray(ix["L0_ffn_bs1"].reshape(16, 128).T),
        "bs3_in": np.ascontiguousarray(ix["L0_ffn_bs3"].reshape(16, 128).T),
        "bs2_in": np.ascontiguousarray(ix["L0_ffn_bs2"].reshape(1, D)),
        "gb_in": np.ascontiguousarray(ix["L0_ffn_gate_b"].reshape(NE, 1)),
        "ropeAT_in": ropeAT,
        "ident_in": np.eye(128, dtype=np.float32),
        "esel_in": np.ascontiguousarray(
            np.repeat(np.eye(NE), 128, axis=1)).astype(ml_dtypes.bfloat16),
    }
    import os
    nc = build_program(stage=int(os.environ.get("KSTAGE", "5")))
    in_maps = []
    for c in range(8):
        t0 = c * TPC
        pos0 = t0 % SEQ
        m = dict(shared)
        m["x_in"] = np.ascontiguousarray(x[t0:t0 + TPC].T)
        m["tq_hi"] = np.ascontiguousarray(tq_hi[pos0:pos0 + TPC].T)
        m["tq_lo"] = np.ascontiguousarray(tq_lo[pos0:pos0 + TPC].T)
        m["tk_hi"] = np.ascontiguousarray(tk_hi[pos0:pos0 + TPC].T)
        m["tk_lo"] = np.ascontiguousarray(tk_lo[pos0:pos0 + TPC].T)
        in_maps.append(m)
    res = run_bass_kernel_spmd(nc, in_maps, core_ids=list(range(8)))
    if os.environ.get("KTAP"):
        import pickle
        taps = {f"{nm}_{c}": np.asarray(res.results[c][nm])
                for c in range(8)
                for nm in ("tap_uh", "tap_ul", "tap_g", "tap_x0",
                           "tap_geb", "tap_hh", "tap_hl")}
        with open("/tmp/ktaps.pkl", "wb") as fh:
            pickle.dump(taps, fh)
    out = np.empty((16384, D), np.float32)
    for c in range(8):
        out[c * TPC:(c + 1) * TPC] = res.results[c]["y_out"].T
    return out.reshape(4, SEQ, D)


# version 24 (final)
# speedup vs baseline: 1.0995x; 1.0995x over previous
import math
import os
import numpy as np
import ml_dtypes
import concourse.bass as bass
import concourse.mybir as mybir
from concourse.bass_utils import run_bass_kernel_spmd
from contextlib import ExitStack
from types import SimpleNamespace

F32 = mybir.dt.float32
BF16 = mybir.dt.bfloat16
ALU = mybir.AluOpType
AF = mybir.ActivationFunctionType
AXX = mybir.AxisListType.X

D = 1024
DC = 8
TPC = 2048
TH = 1024
TTW = 512
NTT = 2
NE = 8
SINT = 2048
SEQ = 4096
EPS = 1e-5
SLOT = 1024
INV_D = 1.0 / D


class Sem:
    def __init__(self, ctx, nc, name):
        self.h = ctx.enter_context(nc.semaphore(name))
        self.n = 0
        self.name = name
        self._w = {}

    def inc(self, inst, k=1):
        inst.then_inc(self.h, k)
        self.n += k
        return self.n

    def wait(self, eng, n=None):
        n = self.n if n is None else n
        assert n <= self.n, f"wait {self.name} {n} > {self.n}"
        if n <= 0:
            return
        k = id(eng)
        if self._w.get(k, -1) >= n:
            return
        eng.wait_ge(self.h, n)
        self._w[k] = n


class WRing:

    def __init__(self, S):
        self.S = S
        nc, ctx = S.nc, S.ctx
        self.nstage = 2
        self.nslot = 4
        self.stage = [ctx.enter_context(
            nc.sbuf_tensor(f"wstage{i}", [128, SLOT], F32))
            for i in range(self.nstage)]
        self.hi = [ctx.enter_context(
            nc.sbuf_tensor(f"whi{i}", [128, SLOT], BF16))
            for i in range(self.nslot)]
        self.lo = [ctx.enter_context(
            nc.sbuf_tensor(f"wlo{i}", [128, SLOT], BF16))
            for i in range(self.nslot)]
        self.fills = []
        self.nfill = 0
        self.nsplit = 0
        self.split_ve = {}
        self.use_te = {}
        self.init_ve = S.s_ve.n

    def queue(self, dram_ap):
        self.fills.append(dram_ap)
        return len(self.fills) - 1

    def _fill_ok(self):
        s = self.nfill
        if s >= len(self.fills):
            return False
        if s - self.nsplit >= self.nstage:
            return False
        if s - self.nstage >= 0 and s - self.nstage >= self.nsplit:
            return False
        return True

    def _emit_fill(self):
        s = self.nfill
        S = self.S
        eng = S.nc.sync
        if s - self.nstage >= 0:
            S.s_ve.wait(eng, self.split_ve[s - self.nstage])
        else:
            S.s_ve.wait(eng, self.init_ve)
        src = self.fills[s]
        dst = self.stage[s % 2][:].rearrange("p (a b) -> p a b",
                                             a=src.shape[1])
        S.s_wdma.inc(eng.dma_start(out=dst, in_=src), 16)
        self.nfill += 1

    def _emit_split(self):
        s = self.nsplit
        S = self.S
        while self.nfill <= s:
            assert self._fill_ok()
            self._emit_fill()
        v = S.nc.vector
        S.s_wdma.wait(v, 16 * (s + 1))
        if s - self.nslot >= 0:
            S.s_te.wait(v, self.use_te[s - self.nslot])
        st = self.stage[s % self.nstage][:]
        v.tensor_copy(self.hi[s % self.nslot][:], st)
        inst = v.tensor_tensor(self.lo[s % self.nslot][:], st,
                               self.hi[s % self.nslot][:], ALU.subtract)
        self.split_ve[s] = S.s_ve.inc(inst)
        self.nsplit += 1

    def _split_ok(self):
        s = self.nsplit
        if s >= len(self.fills):
            return False
        if s - self.nslot >= 0 and (s - self.nslot) not in self.use_te:
            return False
        if s >= self.nfill and not self._fill_ok():
            return False
        return True

    def prefetch(self, ahead=1):
        tgt = self.nsplit + ahead
        while self.nsplit < tgt and self._split_ok():
            self._emit_split()
        while self._fill_ok() and self.nfill <= self.nsplit + 2:
            self._emit_fill()

    def acquire(self, slots):
        hi = max(slots)
        assert hi - min(slots) <= 1 and hi < len(self.fills)
        while self.nsplit <= hi:
            self._emit_split()
        self.prefetch(2)
        while self._fill_ok() and self.nfill <= hi + 3:
            self._emit_fill()
        for s in slots:
            self.S.s_ve.wait(self.S.nc.tensor, self.split_ve[s])

    def view(self, s, a):
        h = self.hi[s % self.nslot][:].rearrange("p (a b) -> p a b", a=a)
        l = self.lo[s % self.nslot][:].rearrange("p (a b) -> p a b", a=a)
        return h, l

    def release(self, slots):
        for s in slots:
            self.use_te[s] = self.S.s_te.n
        self.prefetch(2)


class PsumRing:

    def __init__(self, S, names):
        self.S = S
        self.names = names
        self.i = 0

    def begin(self):
        nm = self.names[self.i % len(self.names)]
        self._nm = nm
        t = self.S.ptok.get(nm)
        if t is not None:
            t[0].wait(self.S.nc.tensor, t[1])
        return self.S.banks[nm]

    def done(self, sem, n):
        self.S.ptok[self._nm] = (sem, n)
        self.i += 1


def p_begin(S, name):
    t = S.ptok.get(name)
    if t is not None:
        t[0].wait(S.nc.tensor, t[1])
    return S.banks[name] if name in getattr(S, "banks", {}) else getattr(S, name)


def p_done(S, name, sem, n):
    S.ptok[name] = (sem, n)


def mm3_group(S, p_ap, chunks):
    t = S.nc.tensor
    seq = []
    for (lh, ll, rh, rl) in chunks:
        seq.append((lh, rh))
        if rl is not None:
            seq.append((lh, rl))
        if ll is not None:
            seq.append((ll, rh))
    n = len(seq)
    inst = None
    for i, (lhsT, rhs) in enumerate(seq):
        inst = t.matmul(p_ap, lhsT, rhs, start=(i == 0), stop=(i == n - 1))
    return inst


def emit_layernorm(S, th):
    nc = S.nc
    t, v, sc = nc.tensor, nc.vector, nc.scalar
    S.s_ve.wait(t, S.x_guard)
    S.s_ve.wait(sc, S.x_guard)
    se_stat = [0, 0]
    for tt in range(NTT):
        base = th * TH + tt * TTW
        xs = [S.X[:, dc, base:base + TTW] for dc in range(DC)]
        p_begin(S, "pS")
        for dc in range(DC):
            inst = t.matmul(S.pS[0:1, 0:TTW], S.ones_col_f[:, 0:1], xs[dc],
                            start=(dc == 0), stop=(dc == DC - 1))
        S.s_te.inc(inst)
        te_sq = {}
        for dc in range(DC):
            if dc >= 2:
                S.s_te.wait(sc, te_sq[dc - 2])
            elif S.sq_guard:
                S.s_te.wait(sc, S.sq_guard)
            inst = sc.activation(S.sq[dc % 2][:], xs[dc], AF.Square)
            se_sq = S.s_se.inc(inst)
            S.s_se.wait(t, se_sq)
            inst = t.matmul(S.pS[0:1, TTW:2 * TTW], S.ones_col_f[:, 0:1],
                            S.sq[dc % 2][:], start=(dc == 0),
                            stop=(dc == DC - 1))
            te_sq[dc] = S.s_te.inc(inst)
        S.sq_guard = te_sq[DC - 1]
        S.s_ve.wait(sc, S.lnstat_guard)
        S.s_te.wait(sc, te_sq[DC - 1])
        inst = sc.copy(S.stat[0:1, tt * 1024:tt * 1024 + 2 * TTW],
                       S.pS[0:1, 0:2 * TTW])
        se_stat[tt] = S.s_se.inc(inst)
        p_done(S, "pS", S.s_se, se_stat[tt])
    for tt in range(NTT):
        base = th * TH + tt * TTW
        ucol = tt * TTW
        xs = [S.X[:, dc, base:base + TTW] for dc in range(DC)]
        sm = S.sm[0:1, 0:TTW]
        sv = S.sv[0:1, 0:TTW]
        sw = S.sw[0:1, 0:TTW]
        si = S.si[0:1, 0:TTW]
        st = S.stat[0:1, tt * 1024:tt * 1024 + 2 * TTW]
        S.s_se.wait(v, se_stat[tt])
        v.tensor_scalar_mul(sm, st[0:1, 0:TTW], INV_D)
        v.tensor_scalar_mul(sv, st[0:1, TTW:2 * TTW], INV_D)
        v.drain()
        v.tensor_mul(sw, sm, sm)
        v.drain()
        v.tensor_sub(sv, sv, sw)
        v.drain()
        inst = v.tensor_scalar_add(sv, sv, EPS)
        ve_r = S.s_ve.inc(inst)
        S.lnstat_guard = ve_r
        S.s_ve.wait(sc, ve_r)
        if S.ln_bc_guard:
            S.s_te.wait(sc, S.ln_bc_guard)
        sc.activation(sw, sv, AF.Ln)
        inst = sc.activation(si, sw, AF.Exp, scale=-0.5)
        se_istd = S.s_se.inc(inst)
        S.s_se.wait(t, se_istd)
        p_begin(S, "pe")
        t.matmul(S.pE[:, 0:TTW], S.ones_row_f[0:1, :], sm,
                 start=True, stop=True)
        p_begin(S, "pf")
        inst = t.matmul(S.pF[:, 0:TTW], S.ones_row_f[0:1, :], si,
                        start=True, stop=True)
        te_bc = S.s_te.inc(inst)
        S.ln_bc_guard = te_bc
        S.s_te.wait(v, te_bc)
        for dc in range(DC):
            v.tensor_sub(S.tmpA[:], xs[dc], S.pE[:, 0:TTW])
            v.tensor_mul(S.tmpB[:], S.tmpA[:], S.pF[:, 0:TTW])
            v.tensor_copy(S.u_hi[:, dc, ucol:ucol + TTW], S.tmpB[:])
            inst = v.tensor_tensor(S.u_lo[:, dc, ucol:ucol + TTW], S.tmpB[:],
                                   S.u_hi[:, dc, ucol:ucol + TTW],
                                   ALU.subtract)
        ve_u = S.s_ve.inc(inst)
        p_done(S, "pe", S.s_ve, ve_u)
        p_done(S, "pf", S.s_ve, ve_u)
        S.u_ready[tt] = ve_u


def emit_gate(S, th):
    nc = S.nc
    t, v, sc = nc.tensor, nc.vector, nc.scalar
    for tt in range(NTT):
        ucol = tt * TTW
        p_begin(S, "pf")
        chunks = []
        for dc in range(DC):
            chunks.append((S.gw_hi[:, dc, :], S.gw_lo[:, dc, :],
                           S.u_hi[:, dc, ucol:ucol + TTW],
                           S.u_lo[:, dc, ucol:ucol + TTW]))
        S.s_ve.wait(t, S.u_ready[tt])
        inst = mm3_group(S, S.pF[0:8, 0:TTW], chunks)
        te_log = S.s_te.inc(inst)
        if S.gate_guard:
            S.s_te.wait(sc, S.gate_guard)
        S.s_te.wait(sc, te_log)
        inst = sc.activation(S.glog[0:8, 0:TTW], S.pF[0:8, 0:TTW], AF.Identity,
                             bias=S.gb[0:8, 0:1])
        se_log = S.s_se.inc(inst)
        S.s_se.wait(t, se_log)
        p_begin(S, "pe")
        for ts in range(4):
            inst = t.matmul(S.pE[:, ts * 8:(ts + 1) * 8],
                            S.glog[0:8, ts * 128:(ts + 1) * 128],
                            S.ident_f[0:8, 0:8], is_transpose=True,
                            start=(ts == 0), stop=(ts == 3))
        te_tr = S.s_te.inc(inst)
        p_done(S, "pf", S.s_se, se_log)
        S.s_ve.wait(sc, S.gate_guard_ve)
        S.s_te.wait(sc, te_tr)
        inst = sc.activation(S.ge[:, 0:32], S.pE[:, 0:32], AF.Exp)
        se_e = S.s_se.inc(inst)
        p_done(S, "pe", S.s_se, se_e)
        if S.gate_guard:
            S.s_te.wait(v, S.gate_guard)
        S.s_se.wait(v, se_e)
        geg = S.ge[:, 0:32].rearrange("p (g e) -> p g e", g=4)
        v.tensor_reduce(S.g_l[:, 0:4], geg, AXX, ALU.add)
        v.tensor_reduce(S.g_m1[:, 0:4], geg, AXX, ALU.max)
        v.drain()
        v.reciprocal(S.g_rl[:, 0:4], S.g_l[:, 0:4])
        for ts in range(4):
            e_ts = S.ge[:, ts * 8:(ts + 1) * 8]
            v.drain()
            v.tensor_scalar(S.g_eq[:, 0:8], e_ts, S.g_m1[:, ts:ts + 1], None,
                            ALU.is_equal)
            v.drain()
            v.scalar_tensor_tensor(S.g_wo[:, 0:8], S.g_eq[:, 0:8], -1e30,
                                   e_ts, ALU.mult, ALU.add)
            v.drain()
            v.tensor_reduce(S.g_m2[:, 0:1], S.g_wo[:, 0:8], AXX, ALU.max)
            v.drain()
            v.tensor_scalar(S.g_eq[:, 0:8], e_ts, S.g_m2[:, 0:1], None,
                            ALU.is_ge)
            v.drain()
            v.tensor_mul(S.g_gu[:, 0:8], e_ts, S.g_eq[:, 0:8])
            v.drain()
            inst = v.tensor_scalar_mul(S.gtm[:, ts * 8:(ts + 1) * 8],
                                       S.g_gu[:, 0:8], S.g_rl[:, ts:ts + 1])
        ve_g = S.s_ve.inc(inst)
        S.gate_guard_ve = ve_g
        S.s_ve.wait(t, ve_g)
        p_begin(S, "pf")
        for ts in range(4):
            inst = t.matmul(S.pF[0:8, ts * 128:(ts + 1) * 128],
                            S.gtm[:, ts * 8:(ts + 1) * 8],
                            S.ident_f[:], is_transpose=True,
                            start=(ts == 0), stop=(ts == 3))
        te_gt = S.s_te.inc(inst)
        S.gate_guard = te_gt
        S.s_te.wait(sc, te_gt)
        inst = sc.copy(S.gT_f[0:8, ucol:ucol + TTW], S.pF[0:8, 0:TTW])
        se_gt = S.s_se.inc(inst)
        p_done(S, "pf", S.s_se, se_gt)
        S.s_se.wait(v, se_gt)
        v.tensor_copy(S.gT_hi[0:8, ucol:ucol + TTW],
                      S.gT_f[0:8, ucol:ucol + TTW])
        inst = v.tensor_tensor(S.gT_lo[0:8, ucol:ucol + TTW],
                               S.gT_f[0:8, ucol:ucol + TTW],
                               S.gT_hi[0:8, ucol:ucol + TTW], ALU.subtract)
        S.gT_ready[tt] = S.s_ve.inc(inst)


def plan_ffn_fills(S, wr, tag):
    idx = {}
    for th in range(2):
        for e in range(NE):
            for m in range(4):
                for hh in range(2):
                    icc = m * 2 + hh
                    idx[(tag, th, "w1", e, icc)] = wr.queue(
                        S.colslot(S.w1t, icc * 128, e))
                    idx[(tag, th, "w3", e, icc)] = wr.queue(
                        S.colslot(S.w3t, icc * 128, e))
                idx[(tag, th, "w2", e, m)] = wr.queue(
                    S.rowslot(S.w2t, m * 256, e))
                wr.queue(S.rowslot(S.w2t, m * 256 + 128, e))
        for m in range(8):
            for hh in range(2):
                icc = m * 2 + hh
                idx[(tag, th, "ws1", icc)] = wr.queue(
                    S.colslot(S.ws1t, icc * 128))
                idx[(tag, th, "ws3", icc)] = wr.queue(
                    S.colslot(S.ws3t, icc * 128))
            idx[(tag, th, "ws2", m)] = wr.queue(S.rowslot(S.ws2t, m * 256))
            wr.queue(S.rowslot(S.ws2t, m * 256 + 128))
    return idx


def emit_h_epilogue(S, pY1, pY3, b1col, b3col, geb, h_hi_sl, h_lo_sl):
    nc = S.nc
    v, sc = nc.vector, nc.scalar
    k = S.s1_i % 2
    s1 = S.s1[k]
    if S.s1_tok[k] is not None:
        S.s_ve.wait(sc, S.s1_tok[k])
    inst = sc.activation(s1[:], pY1, AF.Silu, bias=b1col)
    se_n = S.s_se.inc(inst)
    S.s_se.wait(v, se_n)
    v.scalar_tensor_tensor(S.tmpB[:], pY3, b3col, s1[:], ALU.add, ALU.mult)
    if geb is not None:
        v.drain()
        v.tensor_mul(S.tmpA[:], S.tmpB[:], geb)
        hsrc = S.tmpA[:]
    else:
        hsrc = S.tmpB[:]
    v.tensor_copy(h_hi_sl, hsrc)
    inst = v.tensor_tensor(h_lo_sl, hsrc, h_hi_sl, ALU.subtract)
    ve_n = S.s_ve.inc(inst)
    S.s1_tok[k] = ve_n
    S.s1_i += 1
    return se_n, ve_n


def emit_ffn(S, wr, fidx, tag):
    nc = S.nc
    t, v, sc = nc.tensor, nc.vector, nc.scalar

    skip_exp = bool(os.environ.get("KSKIP_EXP"))
    skip_sh = bool(os.environ.get("KSKIP_SH"))
    for th in range(2):
        emit_layernorm(S, th)
        emit_gate(S, th)
        if S.taps and tag == "ffn1" and th == 0:
            gp = nc.gpsimd
            S.s_ve.wait(gp)
            S.s_gdma.inc(gp.dma_start(out=S.taps["tap_uh"],
                                      in_=S.u_hi.tensor.ap()), 16)
            S.s_gdma.inc(gp.dma_start(out=S.taps["tap_ul"],
                                      in_=S.u_lo.tensor.ap()), 16)
            S.s_gdma.inc(gp.dma_start(out=S.taps["tap_g"],
                                      in_=S.gT_f), 16)
            S.s_gdma.wait(gp)
        xbase = th * TH

        def macro_pass(e, m, nmac):
            ve_h_tt = [0, 0]
            for hh in range(2):
                icc = m * 2 + hh
                if e is not None:
                    sA = fidx[(tag, th, "w1", e, icc)]
                    sB = fidx[(tag, th, "w3", e, icc)]
                else:
                    sA = fidx[(tag, th, "ws1", icc)]
                    sB = fidx[(tag, th, "ws3", icc)]
                wr.acquire([sA, sB])
                w1h, w1l = wr.view(sA, DC)
                w3h, w3l = wr.view(sB, DC)
                for tt in range(NTT):
                    ucol = tt * TTW
                    S.s_ve.wait(t, S.u_ready[tt])
                    ch1 = [(w1h[:, dc, :], w1l[:, dc, :],
                            S.u_hi[:, dc, ucol:ucol + TTW],
                            S.u_lo[:, dc, ucol:ucol + TTW])
                           for dc in range(DC)]
                    ch3 = [(w3h[:, dc, :], w3l[:, dc, :],
                            S.u_hi[:, dc, ucol:ucol + TTW],
                            S.u_lo[:, dc, ucol:ucol + TTW])
                           for dc in range(DC)]
                    pY1 = S.ppAB.begin()
                    mm3_group(S, pY1[:], ch1)
                    pY3 = S.ppCD.begin()
                    i3 = mm3_group(S, pY3[:], ch3)
                    te_n = S.s_te.inc(i3)
                    S.s_te.wait(sc, te_n)
                    S.s_te.wait(v, te_n)
                    if e is not None:
                        b1col = S.b1[:, e * 8 + icc:e * 8 + icc + 1]
                        b3col = S.b3[:, e * 8 + icc:e * 8 + icc + 1]
                        geb = S.geb[:, ucol:ucol + TTW]
                    else:
                        b1col = S.bs1[:, icc:icc + 1]
                        b3col = S.bs3[:, icc:icc + 1]
                        geb = None
                    se_n, ve_n = emit_h_epilogue(
                        S, pY1[:], pY3[:], b1col, b3col, geb,
                        S.h_hi[:, hh, ucol:ucol + TTW],
                        S.h_lo[:, hh, ucol:ucol + TTW])
                    ve_h_tt[tt] = ve_n
                    S.geb_guard = ve_n
                    S.ppAB.done(S.s_se, se_n)
                    S.ppCD.done(S.s_ve, ve_n)
                wr.release([sA, sB])
            if e is not None:
                sC = fidx[(tag, th, "w2", e, m)]
            else:
                sC = fidx[(tag, th, "ws2", m)]
            wr.acquire([sC, sC + 1])
            w2h0, w2l0 = wr.view(sC, 1)
            w2h1, w2l1 = wr.view(sC + 1, 1)
            last_bias = (e is None and m == nmac - 1)
            for tt in range(NTT):
                for oc in range(DC):
                    och = slice(oc * 128, (oc + 1) * 128)
                    ucol = tt * TTW
                    xsl = S.X[:, oc, xbase + ucol:xbase + ucol + TTW]
                    S.s_ve.wait(t, ve_h_tt[tt])
                    ch = [(w2h0[:, 0, och], w2l0[:, 0, och],
                           S.h_hi[:, 0, ucol:ucol + TTW],
                           S.h_lo[:, 0, ucol:ucol + TTW]),
                          (w2h1[:, 0, och], w2l1[:, 0, och],
                           S.h_hi[:, 1, ucol:ucol + TTW],
                           S.h_lo[:, 1, ucol:ucol + TTW])]
                    if last_bias:
                        S.s_ve.wait(t, S.gT_ready[tt])
                        ch.append((S.b2_hi[0:8, och], S.b2_lo[0:8, och],
                                   S.gT_hi[0:8, ucol:ucol + TTW],
                                   S.gT_lo[0:8, ucol:ucol + TTW]))
                        ch.append((S.bs2_hi[0:1, och], S.bs2_lo[0:1, och],
                                   S.ones_row_b[0:1, 0:TTW], None))
                    pp = S.ppAB.begin()
                    inst = mm3_group(S, pp[:], ch)
                    te_n = S.s_te.inc(inst)
                    S.s_te.wait(v, te_n)
                    inst = v.scalar_tensor_tensor(xsl, pp[:], 0.5, xsl,
                                                  ALU.mult, ALU.add)
                    ve_n = S.s_ve.inc(inst)
                    S.x_guard = ve_n
                    S.ppAB.done(S.s_ve, ve_n)
            wr.release([sC, sC + 1])

        for e in range(NE):
            if skip_exp:
                for m in range(4):
                    for hh in range(2):
                        icc = m * 2 + hh
                        wr.acquire([fidx[(tag, th, "w1", e, icc)],
                                    fidx[(tag, th, "w3", e, icc)]])
                        wr.release([fidx[(tag, th, "w1", e, icc)],
                                    fidx[(tag, th, "w3", e, icc)]])
                    sC = fidx[(tag, th, "w2", e, m)]
                    wr.acquire([sC, sC + 1])
                    wr.release([sC, sC + 1])
                continue
            for tt in range(NTT):
                ucol = tt * TTW
                S.s_ve.wait(t, S.gT_ready[tt])
                p_begin(S, "pe")
                t.matmul(S.pE[:, 0:TTW],
                         S.esel[0:8, e * 128:(e + 1) * 128],
                         S.gT_hi[0:8, ucol:ucol + TTW],
                         start=True, stop=False)
                inst = t.matmul(S.pE[:, 0:TTW],
                                S.esel[0:8, e * 128:(e + 1) * 128],
                                S.gT_lo[0:8, ucol:ucol + TTW],
                                start=False, stop=True)
                te_n = S.s_te.inc(inst)
                S.s_ve.wait(sc, S.geb_guard)
                S.s_te.wait(sc, te_n)
                inst = sc.copy(S.geb[:, ucol:ucol + TTW], S.pE[:, 0:TTW])
                se_n = S.s_se.inc(inst)
                p_done(S, "pe", S.s_se, se_n)
            for m in range(4):
                macro_pass(e, m, 4)
                if S.taps and tag == "ffn1" and th == 0 and e == 1 \
                        and m == 0 and os.environ.get("KTAP3"):
                    gp = nc.gpsimd
                    S.s_ve.wait(gp)
                    S.s_se.wait(gp)
                    S.s_gdma.inc(gp.dma_start(out=S.taps["tap_geb"],
                                              in_=S.geb), 16)
                    S.s_gdma.inc(gp.dma_start(out=S.taps["tap_hh"],
                                              in_=S.h_hi.tensor.ap()), 16)
                    S.s_gdma.inc(gp.dma_start(out=S.taps["tap_hl"],
                                              in_=S.h_lo.tensor.ap()), 16)
                    S.s_gdma.wait(gp)
            if S.taps and tag == "ffn1" and th == 0 and e == 0 \
                    and os.environ.get("KTAP2"):
                gp = nc.gpsimd
                S.s_ve.wait(gp)
                S.s_gdma.inc(gp.dma_start(out=S.taps["tap_x0"],
                                          in_=S.X.tensor.ap()), 16)
                S.s_gdma.wait(gp)
        for m in range(8):
            if skip_sh:
                for hh in range(2):
                    icc = m * 2 + hh
                    wr.acquire([fidx[(tag, th, "ws1", icc)],
                                fidx[(tag, th, "ws3", icc)]])
                    wr.release([fidx[(tag, th, "ws1", icc)],
                                fidx[(tag, th, "ws3", icc)]])
                sC = fidx[(tag, th, "ws2", m)]
                wr.acquire([sC, sC + 1])
                wr.release([sC, sC + 1])
                continue
            macro_pass(None, m, 8)


def plan_attn_fills(S, wr):
    idx = {}
    for th in range(2):
        for w, wt in (("wk", S.wkt), ("wv", S.wvt), ("wq", S.wqt)):
            for cv in range(DC):
                idx[("attn", th, w, cv)] = wr.queue(S.colslot(wt, cv * 128))
    return idx


def kv_slot(S, eng):
    k = S.kv_i % 2
    tok = S.kv_tok[k]
    if tok is not None:
        tok[0].wait(eng, tok[1])
    return k


def kv_mark(S, k, sem):
    S.kv_tok[k] = (sem, sem.n)
    S.kv_i += 1


def emit_qkv(S, wr, fidx):
    nc = S.nc
    t, v, sc, g, sy = nc.tensor, nc.vector, nc.scalar, nc.gpsimd, nc.sync
    tload_i = [0]

    def load_t(which_hi, which_lo, dcg, gw0):
        k = tload_i[0] % 2
        if S.t_tok[k] is not None:
            S.s_ve.wait(sy, S.t_tok[k])
        S.s_tdma.inc(sy.dma_start(
            out=S.tbh[k][:], in_=which_hi[dcg * 128:(dcg + 1) * 128,
                                          gw0:gw0 + TTW]), 16)
        n = S.s_tdma.inc(sy.dma_start(
            out=S.tbl[k][:], in_=which_lo[dcg * 128:(dcg + 1) * 128,
                                          gw0:gw0 + TTW]), 16)
        tload_i[0] += 1
        return S.tbh[k], S.tbl[k], k, n

    for th in range(2):
        emit_layernorm(S, th)
        for proj in ("wk", "wv", "wq"):
            for cv in range(DC):
                sA = fidx[("attn", th, proj, cv)]
                wr.acquire([sA])
                wh, wl = wr.view(sA, DC)
                dcg = cv
                for tt in range(NTT):
                    ucol = tt * TTW
                    gw0 = th * TH + ucol
                    S.s_ve.wait(t, S.u_ready[tt])
                    ch = [(wh[:, dc, :], wl[:, dc, :],
                           S.u_hi[:, dc, ucol:ucol + TTW],
                           S.u_lo[:, dc, ucol:ucol + TTW])
                          for dc in range(DC)]
                    pp = S.ppAB.begin()
                    inst = mm3_group(S, pp[:], ch)
                    te_p = S.s_te.inc(inst)
                    if proj == "wv":
                        bcol = S.bv[:, dcg:dcg + 1]
                        if S.vtr_tok:
                            S.s_te.wait(sc, S.vtr_tok)
                        S.s_te.wait(sc, te_p)
                        inst = sc.activation(S.vtmp[:], pp[:], AF.Identity,
                                             bias=bcol)
                        se_v = S.s_se.inc(inst)
                        S.ppAB.done(S.s_se, se_v)
                        S.s_se.wait(t, se_v)
                        p_begin(S, "pf")
                        for tb in range(4):
                            inst = t.matmul(
                                S.pF[:, tb * 128:(tb + 1) * 128],
                                S.vtmp[:, tb * 128:(tb + 1) * 128],
                                S.ident_f[:], is_transpose=True,
                                start=(tb == 0), stop=(tb == 3))
                        te_tr = S.s_te.inc(inst)
                        S.vtr_tok = te_tr
                        S.s_te.wait(v, te_tr)
                        k = kv_slot(S, v)
                        v.tensor_copy(S.kvh[k][:], S.pF[:])
                        inst = v.tensor_tensor(S.kvl[k][:], S.pF[:],
                                               S.kvh[k][:], ALU.subtract)
                        ve_v = S.s_ve.inc(inst)
                        p_done(S, "pf", S.s_ve, ve_v)
                        S.s_ve.wait(g, ve_v)
                        dst = []
                        for base in (0, 2048):
                            r0 = base + gw0
                            dst.append((
                                S.v_sh_hi[dcg, r0:r0 + TTW, :].rearrange(
                                    "(a p) f -> p a f", p=128),
                                S.v_sh_lo[dcg, r0:r0 + TTW, :].rearrange(
                                    "(a p) f -> p a f", p=128)))
                        srch = S.kvh[k][:].rearrange("p (a b) -> p a b", a=4)
                        srcl = S.kvl[k][:].rearrange("p (a b) -> p a b", a=4)
                        with g.If(S.par == 0):
                            g.dma_start(out=dst[0][0], in_=srch) \
                                .then_inc(S.s_gkv.h, 16)
                            g.dma_start(out=dst[0][1], in_=srcl) \
                                .then_inc(S.s_gkv.h, 16)
                        with g.Else():
                            g.dma_start(out=dst[1][0], in_=srch) \
                                .then_inc(S.s_gkv.h, 16)
                            g.dma_start(out=dst[1][1], in_=srcl) \
                                .then_inc(S.s_gkv.h, 16)
                        S.s_gkv.n += 32
                        kv_mark(S, k, S.s_gkv)
                        continue
                    is_q = proj == "wq"
                    bcol = (S.bq if is_q else S.bk)[:, dcg:dcg + 1]
                    kq = S.qk_i % 2
                    if S.qk_tok[kq] is not None:
                        S.s_te.wait(sc, S.qk_tok[kq])
                        S.s_te.wait(v, S.qk_tok[kq])
                    S.s_te.wait(sc, te_p)
                    inst = sc.activation(S.qh_t[kq][:], pp[:], AF.Identity,
                                         bias=bcol)
                    se_q = S.s_se.inc(inst)
                    S.s_te.wait(v, te_p)
                    S.s_se.wait(v, se_q)
                    inst = v.scalar_tensor_tensor(
                        S.ql_t[kq][:], pp[:], bcol, S.qh_t[kq][:],
                        ALU.add, ALU.subtract)
                    ve_q = S.s_ve.inc(inst)
                    S.ppAB.done(S.s_ve, ve_q)
                    S.s_se.wait(t, se_q)
                    S.s_ve.wait(t, ve_q)
                    p_begin(S, "pe")
                    t.matmul(S.pE[:, 0:TTW], S.ropeAT[:], S.qh_t[kq][:],
                             start=True, stop=False)
                    inst = t.matmul(S.pE[:, 0:TTW], S.ropeAT[:],
                                    S.ql_t[kq][:], start=False, stop=True)
                    te_r = S.s_te.inc(inst)
                    S.qk_tok[kq] = te_r
                    S.qk_i += 1
                    tbh, tbl, tk, tdma_n = load_t(
                        S.tq_hi if is_q else S.tk_hi,
                        S.tq_lo if is_q else S.tk_lo, dcg, gw0)
                    S.s_tdma.wait(v, tdma_n)
                    S.s_te.wait(v, te_r)
                    v.tensor_mul(S.tmpA[:], S.pE[:, 0:TTW], tbh[:])
                    v.tensor_mul(S.tmpB[:], S.pE[:, 0:TTW], tbl[:])
                    v.tensor_add(S.tmpA[:], S.tmpA[:], S.tmpB[:])
                    k2 = kv_slot(S, v)
                    v.tensor_copy(S.kvh[k2][:], S.tmpA[:])
                    inst = v.tensor_tensor(S.kvl[k2][:], S.tmpA[:],
                                           S.kvh[k2][:], ALU.subtract)
                    ve_s = S.s_ve.inc(inst)
                    S.t_tok[tk] = ve_s
                    p_done(S, "pe", S.s_ve, ve_s)
                    qb = th * NTT + tt
                    S.s_ve.wait(g, ve_s)
                    if is_q:
                        g.dma_start(
                            out=S.q_sp_hi[qb, dcg * 128:(dcg + 1) * 128, :],
                            in_=S.kvh[k2][:]).then_inc(S.s_gq.h, 16)
                        g.dma_start(
                            out=S.q_sp_lo[qb, dcg * 128:(dcg + 1) * 128, :],
                            in_=S.kvl[k2][:]).then_inc(S.s_gq.h, 16)
                        S.s_gq.n += 32
                        kv_mark(S, k2, S.s_gq)
                        S.qspill_done[qb] = S.s_gq.n
                    else:
                        r = slice(dcg * 128, (dcg + 1) * 128)
                        with g.If(S.par == 0):
                            g.dma_start(out=S.k_sh_hi[r, gw0:gw0 + TTW],
                                        in_=S.kvh[k2][:]).then_inc(
                                S.s_gkv.h, 16)
                            g.dma_start(out=S.k_sh_lo[r, gw0:gw0 + TTW],
                                        in_=S.kvl[k2][:]).then_inc(
                                S.s_gkv.h, 16)
                        with g.Else():
                            g.dma_start(
                                out=S.k_sh_hi[r, 2048 + gw0:2048 + gw0 + TTW],
                                in_=S.kvh[k2][:]).then_inc(S.s_gkv.h, 16)
                            g.dma_start(
                                out=S.k_sh_lo[r, 2048 + gw0:2048 + gw0 + TTW],
                                in_=S.kvl[k2][:]).then_inc(S.s_gkv.h, 16)
                        S.s_gkv.n += 32
                        kv_mark(S, k2, S.s_gkv)
                wr.release([sA])


def emit_scores(S):
    nc = S.nc
    t, v, sc, g, sy = nc.tensor, nc.vector, nc.scalar, nc.gpsimd, nc.sync
    LVL = int(os.environ.get("KSCORE_LVL", "5"))
    S.s_gkv.wait(g)
    S.s_gq.wait(g)
    with g.If(S.par == 0):
        g.dma_start(out=S.kv_flag[0:1, :], in_=S.nonce_sb[0:1, :]).then_inc(
            S.s_gkv.h, 16)
    with g.Else():
        g.dma_start(out=S.kv_flag[1:2, :], in_=S.nonce_sb[0:1, :]).then_inc(
            S.s_gkv.h, 16)
    S.s_gkv.n += 16
    nv = g.value_load(S.nonce_sb[0:1, 0:1])
    spoll = S.ctx.enter_context(nc.semaphore("spoll"))
    with g.register("pollr") as pr:
        g.reg_mov(pr, 0)

        def _poll_cond():
            with g.If(S.par == 0):
                g.dma_start(out=S.pflag_sb[0:1, :],
                            in_=S.kv_flag[1:2, :]).then_inc(spoll, 16)
            with g.Else():
                g.dma_start(out=S.pflag_sb[0:1, :],
                            in_=S.kv_flag[0:1, :]).then_inc(spoll, 16)
            g.reg_add(pr, pr, 16)
            g.wait_ge(spoll, pr)
            fv = g.value_load(S.pflag_sb[0:1, 0:1])
            return fv != nv

        with g.While(_poll_cond):
            g.nop()
    inst = g.nop()
    S.s_bar.inc(inst)

    kst_tok = [0]
    vr_tok = [None, None, None]

    def load_kstage(ck):
        S.s_bar.wait(sy, 1)
        if kst_tok[0]:
            S.s_te.wait(sy, kst_tok[0])
        src_h = S.k_sh_hi[:, ck * 512:(ck + 1) * 512].rearrange(
            "(a p) n -> p a n", p=128)
        src_l = S.k_sh_lo[:, ck * 512:(ck + 1) * 512].rearrange(
            "(a p) n -> p a n", p=128)
        dst_h = S.ksth[:].rearrange("p (a n) -> p a n", a=DC)
        dst_l = S.kstl[:].rearrange("p (a n) -> p a n", a=DC)
        S.s_kdma.inc(sy.dma_start(out=dst_h, in_=src_h), 16)
        return S.s_kdma.inc(sy.dma_start(out=dst_l, in_=src_l), 16)

    def load_vblock(ck, oc, ring_i):
        k = ring_i % 3
        S.s_bar.wait(sy, 1)
        if vr_tok[k] is not None:
            S.s_te.wait(sy, vr_tok[k])
        src_h = S.v_sh_hi[oc, ck * 512:(ck + 1) * 512, :].rearrange(
            "(a p) f -> p a f", p=128)
        src_l = S.v_sh_lo[oc, ck * 512:(ck + 1) * 512, :].rearrange(
            "(a p) f -> p a f", p=128)
        dst_h = S.vrh[k][:].rearrange("p (a f) -> p a f", a=4)
        dst_l = S.vrl[k][:].rearrange("p (a f) -> p a f", a=4)
        S.s_vdma.inc(sy.dma_start(out=dst_h, in_=src_h), 16)
        n = S.s_vdma.inc(sy.dma_start(out=dst_l, in_=src_l), 16)
        return k, n

    qfetch_n = {}

    def fetch_q(qb):
        S.s_bar.wait(sy, 1)
        S.s_gq.wait(sy, S.qspill_done[qb])
        src_h = S.q_sp_hi[qb].rearrange("(a p) n -> p a n", p=128)
        src_l = S.q_sp_lo[qb].rearrange("(a p) n -> p a n", p=128)
        dst_h = S.qsh[:].rearrange("p (a n) -> p a n", a=DC)
        dst_l = S.qsl[:].rearrange("p (a n) -> p a n", a=DC)
        S.s_qdma.inc(sy.dma_start(out=dst_h, in_=src_h), 16)
        qfetch_n[qb] = S.s_qdma.inc(
            sy.dma_start(out=dst_l, in_=src_l), 16)

    ck_load_n = {}
    vring_i = [0]
    if LVL == 0:
        return
    for qb in range(int(os.environ.get("KQB", "4"))):
        if qb > 0:
            S.s_te.wait(sy)
        fetch_q(qb)
        if qb == 0:
            ck_load_n[0] = load_kstage(0)
        qsh = S.qsh[:].rearrange("p (a n) -> p a n", a=DC)
        qsl = S.qsl[:].rearrange("p (a n) -> p a n", a=DC)
        ksth = S.ksth[:].rearrange("p (a n) -> p a n", a=DC)
        kstl = S.kstl[:].rearrange("p (a n) -> p a n", a=DC)
        p_begin(S, "pS")
        denom_first = [True]
        denom_last = [None]
        if LVL == 1:
            for ck in range(1, 8):
                ck_load_n[ck] = load_kstage(ck)
            continue
        for ck in range(8):
            S.s_kdma.wait(t, ck_load_n[ck])
            S.s_qdma.wait(t, qfetch_n[qb])
            for kb in range(4):
                ko = slice(kb * 128, (kb + 1) * 128)
                ch = [(ksth[:, dc, ko], kstl[:, dc, ko],
                       qsh[:, dc, :], qsl[:, dc, :]) for dc in range(DC)]
                pp = S.ppAB.begin()
                inst = mm3_group(S, pp[:], ch)
                te_s = S.s_te.inc(inst)
                S.s_te.wait(sc, te_s)
                kpp = S.pt_i % 2
                if S.pt_tok[kpp] is not None:
                    S.s_ve.wait(sc, S.pt_tok[kpp])
                if S.den_tok[kpp] is not None:
                    S.s_te.wait(sc, S.den_tok[kpp])
                inst = sc.activation(S.ptmp[kpp][:], pp[:], AF.Exp)
                se_e = S.s_se.inc(inst)
                S.ppAB.done(S.s_se, se_e)
                S.s_se.wait(v, se_e)
                v.tensor_copy(S.pT_hi[:, kb, :], S.ptmp[kpp][:])
                inst = v.tensor_tensor(S.pT_lo[:, kb, :], S.ptmp[kpp][:],
                                       S.pT_hi[:, kb, :], ALU.subtract)
                ve_p = S.s_ve.inc(inst)
                S.pt_tok[kpp] = ve_p
                S.pt_i += 1
                if LVL >= 3:
                    S.s_se.wait(t, se_e)
                    denom_last[0] = t.matmul(
                        S.pS[0:1, 0:TTW], S.ones_col_f[:, 0:1],
                        S.ptmp[kpp][:], start=denom_first[0],
                        stop=(ck == 7 and kb == 3))
                    denom_first[0] = False
                    S.den_tok[kpp] = S.s_te.inc(denom_last[0])
            ve_p_last = ve_p
            if LVL >= 4:
                vloads = {}
                vloads[0] = load_vblock(ck, 0, vring_i[0]); vring_i[0] += 1
                vloads[1] = load_vblock(ck, 1, vring_i[0]); vring_i[0] += 1
            kst_tok[0] = S.s_te.n
            if ck < 7:
                ck_load_n[ck + 1] = load_kstage(ck + 1)
            elif qb < 3:
                ck_load_n[0] = load_kstage(0)
            if LVL < 4:
                continue
            for oc in range(DC):
                if oc + 2 < DC:
                    vloads[oc + 2] = load_vblock(ck, oc + 2, vring_i[0])
                    vring_i[0] += 1
                vk, vn = vloads[oc]
                S.s_ve.wait(t, ve_p_last)
                S.s_vdma.wait(t, vn)
                vh = S.vrh[vk][:].rearrange("p (a f) -> p a f", a=4)
                vl = S.vrl[vk][:].rearrange("p (a f) -> p a f", a=4)
                ch = [(vh[:, a, :], vl[:, a, :], S.pT_hi[:, a, :],
                       S.pT_lo[:, a, :]) for a in range(4)]
                pp = S.ppCD.begin()
                inst = mm3_group(S, pp[:], ch)
                te_pv = S.s_te.inc(inst)
                vr_tok[vk] = te_pv
                if ck == 0:
                    if S.oacc_guard:
                        S.s_ve.wait(sc, S.oacc_guard)
                    S.s_te.wait(sc, te_pv)
                    inst = sc.copy(S.oacc[:, oc, :], pp[:])
                    n = S.s_se.inc(inst)
                    S.ppCD.done(S.s_se, n)
                else:
                    S.s_te.wait(v, te_pv)
                    inst = v.tensor_add(S.oacc[:, oc, :], S.oacc[:, oc, :],
                                        pp[:])
                    n = S.s_ve.inc(inst)
                    S.ppCD.done(S.s_ve, n)
        if LVL < 5:
            if denom_last[0] is not None and (LVL >= 3):
                te_d = S.s_te.inc(denom_last[0])
                p_done(S, "pS", S.s_te, te_d)
            continue
        KTAIL = int(os.environ.get("KTAIL", "3"))
        te_den = S.den_tok[(S.pt_i - 1) % 2]
        S.s_te.wait(v, te_den)
        v.tensor_copy(S.sm[0:1, 0:TTW], S.pS[0:1, 0:TTW])
        v.drain()
        inst = v.reciprocal(S.sv[0:1, 0:TTW], S.sm[0:1, 0:TTW])
        ve_r = S.s_ve.inc(inst)
        p_done(S, "pS", S.s_ve, ve_r)
        if KTAIL < 2:
            S.oacc_guard = 0
            continue
        S.s_ve.wait(t, ve_r)
        p_begin(S, "pe")
        inst = t.matmul(S.pE[:, 0:TTW], S.ones_row_f[0:1, :],
                        S.sv[0:1, 0:TTW], start=True, stop=True)
        te_b = S.s_te.inc(inst)
        if KTAIL < 3:
            p_done(S, "pe", S.s_te, te_b)
            S.oacc_guard = 0
            continue
        S.s_te.wait(v, te_b)
        qcol = qb * TTW
        inst = None
        for oc in range(DC):
            v.tensor_mul(S.tmpA[:], S.oacc[:, oc, :], S.pE[:, 0:TTW])
            inst = v.tensor_add(S.X[:, oc, qcol:qcol + TTW],
                                S.X[:, oc, qcol:qcol + TTW], S.tmpA[:])
        ve_x = S.s_ve.inc(inst)
        S.x_guard = ve_x
        p_done(S, "pe", S.s_ve, ve_x)
        S.oacc_guard = ve_x


def build_program(stage=5):
    nc = bass.Bass("TRN2", target_bir_lowering=False)
    S = SimpleNamespace()
    S.nc = nc

    def din(name, shape, dt=F32):
        return nc.dram_tensor(name, shape, dt, kind="ExternalInput").ap()

    S.x_in = din("x_in", [D, TPC])
    S.taps = {}
    if os.environ.get("KTAP"):
        for nm, shape, dt in [("tap_uh", [128, DC * TH], BF16),
                              ("tap_ul", [128, DC * TH], BF16),
                              ("tap_g", [NE, TH], F32),
                              ("tap_x0", [128, DC * TPC], F32),
                              ("tap_geb", [128, TH], F32),
                              ("tap_hh", [128, 2 * TH], BF16),
                              ("tap_hl", [128, 2 * TH], BF16)]:
            S.taps[nm] = nc.dram_tensor(nm, shape, dt,
                                        kind="ExternalOutput").ap()
    S.y_out = nc.dram_tensor("y_out", [D, TPC], F32,
                             kind="ExternalOutput").ap()
    S.wqt = din("wqt", [D, D])
    S.wkt = din("wkt", [D, D])
    S.wvt = din("wvt", [D, D])
    S.w1t = din("w1t", [NE, D, D])
    S.w3t = din("w3t", [NE, D, D])
    S.w2t = din("w2t", [NE, D, D])
    S.ws1t = din("ws1t", [D, SINT])
    S.ws3t = din("ws3t", [D, SINT])
    S.ws2t = din("ws2t", [SINT, D])
    S.gwt = din("gwt", [D, NE])
    S.bq_in = din("bq_in", [128, DC])
    S.bk_in = din("bk_in", [128, DC])
    S.bv_in = din("bv_in", [128, DC])
    S.b1_in = din("b1_in", [128, 64])
    S.b3_in = din("b3_in", [128, 64])
    S.b2_in = din("b2_in", [NE, D])
    S.bs1_in = din("bs1_in", [128, 16])
    S.bs3_in = din("bs3_in", [128, 16])
    S.bs2_in = din("bs2_in", [1, D])
    S.gb_in = din("gb_in", [NE, 1])
    S.tq_hi = din("tq_hi", [D, TPC])
    S.tq_lo = din("tq_lo", [D, TPC])
    S.tk_hi = din("tk_hi", [D, TPC])
    S.tk_lo = din("tk_lo", [D, TPC])
    S.ropeAT_in = din("ropeAT_in", [128, 128], BF16)
    S.ident_in = din("ident_in", [128, 128])
    S.esel_in = din("esel_in", [NE, NE * 128], BF16)

    S.q_sp_hi = nc.dram_tensor("q_sp_hi", [4, D, TTW], BF16).ap()
    S.q_sp_lo = nc.dram_tensor("q_sp_lo", [4, D, TTW], BF16).ap()
    S.k_sh_hi = nc.dram_tensor("k_sh_hi", [D, SEQ], BF16,
                               addr_space="Shared").ap()
    S.k_sh_lo = nc.dram_tensor("k_sh_lo", [D, SEQ], BF16,
                               addr_space="Shared").ap()
    S.v_sh_hi = nc.dram_tensor("v_sh_hi", [DC, SEQ, 128], BF16,
                               addr_space="Shared").ap()
    S.v_sh_lo = nc.dram_tensor("v_sh_lo", [DC, SEQ, 128], BF16,
                               addr_space="Shared").ap()
    S.bar_in = nc.dram_tensor("bar_in", [1, 32], F32).ap()
    S.bar_out = nc.dram_tensor("bar_out", [2, 32], F32).ap()
    S.nonce_in = din("nonce_in", [1, 32], mybir.dt.int32)
    S.kv_flag = nc.dram_tensor("kv_flag", [2, 32], mybir.dt.int32,
                               addr_space="Shared").ap()

    def colslot(wt, c0, e=None):
        ap = wt if e is None else wt[e]
        return ap[:, c0:c0 + 128].rearrange("(a p) n -> p a n", p=128)

    def rowslot(wt, r0, e=None):
        ap = wt if e is None else wt[e]
        return ap[r0:r0 + 128, :].rearrange("(a p) n -> p a n", p=128)

    S.colslot = colslot
    S.rowslot = rowslot

    with ExitStack() as ctx:
        S.ctx = ctx

        def sb(name, shape, dt=F32, c=None):
            return (c or ctx).enter_context(
                nc.sbuf_tensor(name, shape, dt))[:]

        S.X = sb("X", [128, DC * TPC]).rearrange("p (a n) -> p a n", a=DC)
        S.tmpA = sb("tmpA", [128, TTW])
        S.tmpB = sb("tmpB", [128, TTW])
        S.sq = [sb(f"sq{i}", [128, TTW]) for i in range(2)]
        S.s1 = [sb(f"s1_{i}", [128, TTW]) for i in range(2)]
        S.stat = sb("stat", [1, 2048])
        S.sm = sb("sm", [1, TTW])
        S.sv = sb("sv", [1, TTW])
        S.sw = sb("sw", [1, TTW])
        S.si = sb("si", [1, TTW])
        S.glog = sb("glog", [NE, TTW])
        S.ge = sb("gexp", [128, 32])
        S.gtm = sb("gtm", [128, 32])
        S.g_l = sb("g_l", [128, 4])
        S.g_m1 = sb("g_m1", [128, 4])
        S.g_rl = sb("g_rl", [128, 4])
        S.g_m2 = sb("g_m2", [128, 1])
        S.g_eq = sb("g_eq", [128, 8])
        S.g_wo = sb("g_wo", [128, 8])
        S.g_gu = sb("g_gu", [128, 8])
        S.zeros8 = sb("zeros8", [128, 8])
        S.gT_f = sb("gT_f", [NE, TH])
        S.gT_hi = sb("gT_hi", [NE, TH], BF16)
        S.gT_lo = sb("gT_lo", [NE, TH], BF16)
        S.gw_hi = sb("gw_hi", [128, DC * 8], BF16).rearrange(
            "p (a n) -> p a n", a=DC)
        S.gw_lo = sb("gw_lo", [128, DC * 8], BF16).rearrange(
            "p (a n) -> p a n", a=DC)
        S.bq = sb("bq", [128, DC])
        S.bk = sb("bk", [128, DC])
        S.bv = sb("bv", [128, DC])
        S.b1 = sb("b1", [128, 64])
        S.b3 = sb("b3", [128, 64])
        S.bs1 = sb("bs1", [128, 16])
        S.bs3 = sb("bs3", [128, 16])
        S.gb = sb("gb", [NE, 1])
        S.b2_hi = sb("b2_hi", [NE, D], BF16)
        S.b2_lo = sb("b2_lo", [NE, D], BF16)
        S.bs2_hi = sb("bs2_hi", [1, D], BF16)
        S.bs2_lo = sb("bs2_lo", [1, D], BF16)
        S.ones_col_f = sb("ones_col_f", [128, 1])
        S.ones_col_b = sb("ones_col_b", [128, 1], BF16)
        S.ones_row_f = sb("ones_row_f", [1, 128])
        S.ones_row_b = sb("ones_row_b", [1, TTW], BF16)
        S.ropeAT = sb("ropeAT", [128, 128], BF16)
        S.nonce_sb = sb("nonce_sb", [1, 32], mybir.dt.int32)
        S.pflag_sb = sb("pflag_sb", [1, 32], mybir.dt.int32)
        S.ident_f = sb("ident_f", [128, 128])
        S.esel = sb("esel", [NE, NE * 128], BF16)

        S.banks = {}
        for nm in ("pa", "pb", "pc", "pd", "pe", "pf"):
            S.banks[nm] = ctx.enter_context(
                nc.psum_tensor(nm, [128, TTW], F32))[:]
        S.pE = S.banks["pe"]
        S.pF = S.banks["pf"]
        S.pS = ctx.enter_context(nc.psum_tensor("ps", [1, 1024], F32))[:]

        for nm in ("s_wdma", "s_tdma", "s_kdma", "s_vdma", "s_qdma",
                   "s_gdma", "s_gkv", "s_gq", "s_te", "s_ve", "s_se",
                   "s_bar"):
            setattr(S, nm, Sem(ctx, nc, nm))

        S.ptok = {}
        S.ppAB = PsumRing(S, ["pa", "pb", "pe"])
        S.ppCD = PsumRing(S, ["pc", "pd", "pf"])
        S.u_ready = [0, 0]
        S.gT_ready = [0, 0]
        S.s1_i = 0
        S.s1_tok = [None, None]
        S.qk_i = 0
        S.qk_tok = [None, None]
        S.kv_i = 0
        S.kv_tok = [None, None]
        S.t_tok = [None, None]
        S.pt_i = 0
        S.pt_tok = [None, None]
        S.den_tok = [None, None]
        S.qspill_done = {}
        S.sq_guard = 0
        S.gate_guard = 0
        S.vtr_tok = 0
        S.oacc_guard = 0
        S.x_guard = 0
        S.lnstat_guard = 0
        S.ln_bc_guard = 0
        S.geb_guard = 0
        S.gate_guard_ve = 0

        with nc.Block():
            g, v = nc.gpsimd, nc.vector
            pid = g.partition_id()
            S.par = g.compute_val(pid % 2)
            with ExitStack() as c0:
                b2f = sb("b2_f", [NE, D], c=c0)
                bs2f = sb("bs2_f", [1, D], c=c0)
                gwstage = sb("gwstage", [128, DC * 8], c=c0)
                for dst, src in [
                    (S.X, S.x_in.rearrange("(a p) n -> p a n", p=128)),
                    (gwstage.rearrange("p (a n) -> p a n", a=DC),
                     S.gwt.rearrange("(a p) n -> p a n", p=128)),
                    (S.bq, S.bq_in), (S.bk, S.bk_in), (S.bv, S.bv_in),
                    (S.b1, S.b1_in), (S.b3, S.b3_in),
                    (S.bs1, S.bs1_in), (S.bs3, S.bs3_in),
                    (S.gb, S.gb_in), (b2f, S.b2_in), (bs2f, S.bs2_in),
                    (S.ropeAT, S.ropeAT_in), (S.ident_f, S.ident_in),
                    (S.nonce_sb, S.nonce_in),
                    (S.esel, S.esel_in),
                ]:
                    S.s_gdma.inc(g.dma_start(out=dst, in_=src), 16)
                v.memset(S.ones_col_f, 1.0)
                v.memset(S.ones_col_b, 1.0)
                v.memset(S.ones_row_f, 1.0)
                v.memset(S.ones_row_b, 1.0)
                v.memset(S.zeros8, 0.0)
                S.s_gdma.wait(v)
                v.tensor_copy(S.gw_hi.tensor.ap(), gwstage)
                v.tensor_tensor(S.gw_lo.tensor.ap(), gwstage,
                                S.gw_hi.tensor.ap(), ALU.subtract)
                v.tensor_copy(S.b2_hi, b2f)
                v.tensor_tensor(S.b2_lo, b2f, S.b2_hi, ALU.subtract)
                v.tensor_copy(S.bs2_hi, bs2f)
                inst = v.tensor_tensor(S.bs2_lo, bs2f, S.bs2_hi,
                                       ALU.subtract)
                ve0 = S.s_ve.inc(inst)
                S.s_ve.wait(nc.tensor, ve0)
                S.s_gdma.wait(nc.tensor)
                S.s_gdma.wait(nc.scalar)
                S.s_gdma.wait(nc.sync)
            wr = WRing(S)
            f1 = plan_ffn_fills(S, wr, "ffn1") if stage >= 2 else None
            fa = plan_attn_fills(S, wr) if stage >= 3 else None
            f2 = plan_ffn_fills(S, wr, "ffn2") if stage >= 5 else None

            with ExitStack() as c1:
                S.u_hi = sb("u_hi1", [128, DC * TH], BF16, c=c1).rearrange(
                    "p (a n) -> p a n", a=DC)
                S.u_lo = sb("u_lo1", [128, DC * TH], BF16, c=c1).rearrange(
                    "p (a n) -> p a n", a=DC)
                S.h_hi = sb("h_hi1", [128, 2 * TH], BF16, c=c1).rearrange(
                    "p (a n) -> p a n", a=2)
                S.h_lo = sb("h_lo1", [128, 2 * TH], BF16, c=c1).rearrange(
                    "p (a n) -> p a n", a=2)
                S.geb = sb("geb1", [128, TH], c=c1)
                S.vtmp = sb("vtmp", [128, TTW], c=c1)
                S.qh_t = [sb(f"qh_t{i}", [128, TTW], BF16, c=c1)
                          for i in range(2)]
                S.ql_t = [sb(f"ql_t{i}", [128, TTW], BF16, c=c1)
                          for i in range(2)]
                S.kvh = [sb(f"kvh{i}", [128, TTW], BF16, c=c1)
                         for i in range(2)]
                S.kvl = [sb(f"kvl{i}", [128, TTW], BF16, c=c1)
                         for i in range(2)]
                S.tbh = [sb(f"tbh{i}", [128, TTW], c=c1) for i in range(2)]
                S.tbl = [sb(f"tbl{i}", [128, TTW], c=c1) for i in range(2)]
                if stage >= 2:
                    emit_ffn(S, wr, f1, "ffn1")
                if stage >= 3:
                    emit_qkv(S, wr, fa)

            with ExitStack() as c2:
                S.qsh = sb("qsh", [128, DC * TTW], BF16, c=c2)
                S.qsl = sb("qsl", [128, DC * TTW], BF16, c=c2)
                S.ksth = sb("ksth", [128, DC * TTW], BF16, c=c2)
                S.kstl = sb("kstl", [128, DC * TTW], BF16, c=c2)
                S.vrh = [sb(f"vrh{i}", [128, TTW], BF16, c=c2)
                         for i in range(3)]
                S.vrl = [sb(f"vrl{i}", [128, TTW], BF16, c=c2)
                         for i in range(3)]
                S.pT_hi = sb("pT_hi", [128, 4 * TTW], BF16, c=c2).rearrange(
                    "p (a n) -> p a n", a=4)
                S.pT_lo = sb("pT_lo", [128, 4 * TTW], BF16, c=c2).rearrange(
                    "p (a n) -> p a n", a=4)
                S.ptmp = [sb(f"ptmp{i}", [128, TTW], c=c2) for i in range(2)]
                S.oacc = sb("oacc", [128, DC * TTW], c=c2).rearrange(
                    "p (a n) -> p a n", a=DC)
                if stage >= 4:
                    emit_scores(S)

            with ExitStack() as c3:
                S.u_hi = sb("u_hi2", [128, DC * TH], BF16, c=c3).rearrange(
                    "p (a n) -> p a n", a=DC)
                S.u_lo = sb("u_lo2", [128, DC * TH], BF16, c=c3).rearrange(
                    "p (a n) -> p a n", a=DC)
                S.h_hi = sb("h_hi2", [128, 2 * TH], BF16, c=c3).rearrange(
                    "p (a n) -> p a n", a=2)
                S.h_lo = sb("h_lo2", [128, 2 * TH], BF16, c=c3).rearrange(
                    "p (a n) -> p a n", a=2)
                S.geb = sb("geb2", [128, TH], c=c3)
                if stage >= 5:
                    emit_ffn(S, wr, f2, "ffn2")

            S.s_ve.wait(g)
            S.s_te.wait(g)
            S.s_se.wait(g)
            out_ap = S.y_out.rearrange("(a p) n -> p a n", p=128)
            S.s_gdma.inc(g.dma_start(out=out_ap, in_=S.X), 16)
            S.s_gdma.wait(g)
    return nc


def _rope_tables():
    pos = np.arange(SEQ, dtype=np.float64)
    inv = 10000.0 ** (np.arange(0, 512, 2, dtype=np.float64) / 512.0)
    radians = pos[:, None] / inv
    sincos = np.stack((np.sin(radians), np.cos(radians)),
                      axis=-1).reshape(SEQ, 512)
    return np.repeat(sincos, 2, axis=-1)


def kernel(**inputs):
    ix = {k: np.asarray(v) for k, v in inputs.items()}
    x = np.ascontiguousarray(ix["L0_attn_hidden_states"].reshape(-1, D))
    t64 = _rope_tables()
    tq64 = t64 / math.sqrt(512.0)
    tq_hi = tq64.astype(np.float32)
    tq_lo = (tq64 - tq_hi.astype(np.float64)).astype(np.float32)
    tk_hi = t64.astype(np.float32)
    tk_lo = (t64 - tk_hi.astype(np.float64)).astype(np.float32)

    def T(a):
        return np.ascontiguousarray(np.asarray(a).T)

    ropeA = np.zeros((128, 128), np.float32)
    for i in range(64):
        ropeA[2 * i, 2 * i] = 1.0
        ropeA[2 * i, 2 * i + 1] = -1.0
        ropeA[2 * i + 1, 2 * i] = 1.0
        ropeA[2 * i + 1, 2 * i + 1] = 1.0
    ropeAT = np.ascontiguousarray(ropeA.T).astype(ml_dtypes.bfloat16)

    shared = {
        "wqt": T(ix["L0_attn_Wq"]), "wkt": T(ix["L0_attn_Wk"]),
        "wvt": T(ix["L0_attn_Wv"]),
        "w1t": np.ascontiguousarray(np.transpose(ix["L0_ffn_w1"], (0, 2, 1))),
        "w3t": np.ascontiguousarray(np.transpose(ix["L0_ffn_w3"], (0, 2, 1))),
        "w2t": np.ascontiguousarray(np.transpose(ix["L0_ffn_w2"], (0, 2, 1))),
        "ws1t": T(ix["L0_ffn_ws1"]), "ws3t": T(ix["L0_ffn_ws3"]),
        "ws2t": T(ix["L0_ffn_ws2"]),
        "gwt": T(ix["L0_ffn_gate_w"]),
        "bq_in": np.ascontiguousarray(ix["L0_attn_bq"].reshape(DC, 128).T),
        "bk_in": np.ascontiguousarray(ix["L0_attn_bk"].reshape(DC, 128).T),
        "bv_in": np.ascontiguousarray(ix["L0_attn_bv"].reshape(DC, 128).T),
        "b1_in": np.ascontiguousarray(
            ix["L0_ffn_b1"].reshape(NE, DC, 128).transpose(2, 0, 1)
            .reshape(128, 64)),
        "b3_in": np.ascontiguousarray(
            ix["L0_ffn_b3"].reshape(NE, DC, 128).transpose(2, 0, 1)
            .reshape(128, 64)),
        "b2_in": np.ascontiguousarray(ix["L0_ffn_b2"]),
        "bs1_in": np.ascontiguousarray(ix["L0_ffn_bs1"].reshape(16, 128).T),
        "bs3_in": np.ascontiguousarray(ix["L0_ffn_bs3"].reshape(16, 128).T),
        "bs2_in": np.ascontiguousarray(ix["L0_ffn_bs2"].reshape(1, D)),
        "gb_in": np.ascontiguousarray(ix["L0_ffn_gate_b"].reshape(NE, 1)),
        "ropeAT_in": ropeAT,
        "ident_in": np.eye(128, dtype=np.float32),
        "nonce_in": None,
        "esel_in": np.ascontiguousarray(
            np.repeat(np.eye(NE), 128, axis=1)).astype(ml_dtypes.bfloat16),
    }
    import os
    nc = build_program(stage=int(os.environ.get("KSTAGE", "5")))
    shared["nonce_in"] = np.random.randint(
        1, 2**31 - 1, size=(1, 32), dtype=np.int32)
    in_maps = []
    for c in range(8):
        t0 = c * TPC
        pos0 = t0 % SEQ
        m = dict(shared)
        m["x_in"] = np.ascontiguousarray(x[t0:t0 + TPC].T)
        m["tq_hi"] = np.ascontiguousarray(tq_hi[pos0:pos0 + TPC].T)
        m["tq_lo"] = np.ascontiguousarray(tq_lo[pos0:pos0 + TPC].T)
        m["tk_hi"] = np.ascontiguousarray(tk_hi[pos0:pos0 + TPC].T)
        m["tk_lo"] = np.ascontiguousarray(tk_lo[pos0:pos0 + TPC].T)
        in_maps.append(m)
    res = run_bass_kernel_spmd(nc, in_maps, core_ids=list(range(8)))
    if os.environ.get("KTAP"):
        import pickle
        taps = {f"{nm}_{c}": np.asarray(res.results[c][nm])
                for c in range(8)
                for nm in ("tap_uh", "tap_ul", "tap_g", "tap_x0",
                           "tap_geb", "tap_hh", "tap_hl")}
        with open("/tmp/ktaps.pkl", "wb") as fh:
            pickle.dump(taps, fh)
    out = np.empty((16384, D), np.float32)
    for c in range(8):
        out[c * TPC:(c + 1) * TPC] = res.results[c]["y_out"].T
    return out.reshape(4, SEQ, D)
